# version 08
# baseline (speedup 1.0000x reference)
import contextlib
import numpy as np
import ml_dtypes
import concourse.bass as bass
import concourse.mybir as mybir

F32 = mybir.dt.float32
BF16 = mybir.dt.bfloat16
AF = mybir.ActivationFunctionType
ALU = mybir.AluOpType
AX = mybir.AxisListType

B, S, H = 4, 2048, 4096
TOK = 1024
T = 512
NCH = H // 128
D = 1024
EPS = 1e-5
INT = 8192
LAT = 2048
NE = 8

bf16 = ml_dtypes.bfloat16


class Sem:
    def __init__(self, es, nc, name):
        self.h = es.enter_context(nc.semaphore(name))
        self.n = 0

    def inc(self, instr, k=1):
        instr.then_inc(self.h, k)
        self.n += k
        return self.n


class Ring:
    def __init__(self, es, nc, name, buf, nslots):
        self.buf, self.nslots = buf, nslots
        self.ls = [Sem(es, nc, f"{name}_l{i}") for i in range(nslots)]
        self.nload = 0

    def load(self, g, src_ap, free=()):
        i = self.nload
        for (sem, val) in free:
            g.wait_ge(sem.h, val)
        self.ls[i % self.nslots].inc(
            g.dma_start(self.buf[:, i % self.nslots], src_ap), 16)
        self.nload += 1

    def wait(self, i):
        return (self.ls[i % self.nslots], 16 * (i // self.nslots + 1))


def build_program(stage="full", dbg=()):
    nc = bass.Bass("TRN2", target_bir_lowering=False, debug=False)

    xT = nc.dram_tensor("xT", [H, S], F32, kind="ExternalInput")
    Wqkv = nc.dram_tensor("Wqkv", [24, 128, 32, 128], F32, kind="ExternalInput")
    Wo = nc.dram_tensor("Wo", [32, 128, 8, 128], F32, kind="ExternalInput")
    Wup = nc.dram_tensor("Wup", [64, 128, 32, 128], F32, kind="ExternalInput")
    Wdn = nc.dram_tensor("Wdn", [64, 128, 32, 128], F32, kind="ExternalInput")
    Wf1 = nc.dram_tensor("Wf1", [16, 128, 32, 128], F32, kind="ExternalInput")
    Wf2 = nc.dram_tensor("Wf2", [32, 128, 16, 128], F32, kind="ExternalInput")
    Wex = nc.dram_tensor("Wex", [16, 128, 16, 128], F32, kind="ExternalInput")
    Wg = nc.dram_tensor("Wg", [128, NCH, NE], F32, kind="ExternalInput")
    bq_c = nc.dram_tensor("bq_c", [128, 8], F32, kind="ExternalInput")
    bv_c = nc.dram_tensor("bv_c", [128, 8], F32, kind="ExternalInput")
    bo_c = nc.dram_tensor("bo_c", [128, 32], F32, kind="ExternalInput")
    gam_c = nc.dram_tensor("gam_c", [128, 32], F32, kind="ExternalInput")
    bet_c = nc.dram_tensor("bet_c", [128, 32], F32, kind="ExternalInput")
    ones_bf = nc.dram_tensor("ones_bf", [128, 128], BF16, kind="ExternalInput")
    ident_bf = nc.dram_tensor("ident_bf", [128, 128], BF16, kind="ExternalInput")
    onesrow = nc.dram_tensor("onesrow", [1, 128], F32, kind="ExternalInput")
    eps_in = nc.dram_tensor("eps_in", [128, 1], F32, kind="ExternalInput")

    outT = nc.dram_tensor("outT", [H, TOK], F32, kind="ExternalOutput")
    qkv_scr = nc.dram_tensor("qkv_scr", [24, 128, 32, 128], BF16, kind="Internal")
    wo_scr = nc.dram_tensor("wo_scr", [32, 128, 8, 128], BF16, kind="Internal")
    f1_scr = nc.dram_tensor("f1_scr", [16, 128, 32, 128], BF16, kind="Internal")
    ex_scr = nc.dram_tensor("ex_scr", [16, 128, 16, 128], BF16, kind="Internal")
    up_scr = nc.dram_tensor("up_scr", [64, 128, 32, 128], BF16, kind="Internal")
    f2_scr = nc.dram_tensor("f2_scr", [32, 128, 16, 128], BF16, kind="Internal")
    dn_scr = nc.dram_tensor("dn_scr", [64, 128, 32, 128], BF16, kind="Internal")
    y_scr = nc.dram_tensor("y_scr", [2, NCH, 128, T], F32, kind="Internal")
    r_scr = nc.dram_tensor("r_scr", [T], F32, kind="Internal")
    s_scr = nc.dram_tensor("s_scr", [T], F32, kind="Internal")

    dbg_t = {}
    def _d(name, shape, dt):
        if name in dbg:
            dbg_t[name] = nc.dram_tensor("dbg_" + name, shape, dt,
                                         kind="ExternalOutput")
    _d("K", [8, 128, S], BF16)
    _d("V", [16, 128, D], BF16)
    _d("q", [8, 128, TOK], BF16)
    _d("ao", [2, 8, 128, T], BF16)
    _d("o", [2, NCH, 128, T], BF16)
    _d("y", [2, NCH, 128, T], F32)
    _d("xc3", [NCH, 128, T], BF16)
    _d("srep", [2, 128, T], F32)
    _d("lat", [16, 128, T], BF16)
    _d("moe", [16, 128, T], BF16)
    _d("hh", [16, 128, T], BF16)
    _d("v", [NCH, 128, T], BF16)

    es = contextlib.ExitStack()
    kv_es = contextlib.ExitStack()
    p1_es = contextlib.ExitStack()
    p2_es = contextlib.ExitStack()
    att_es = contextlib.ExitStack()
    ffn_es = contextlib.ExitStack()

    def sbt(stack, name, shape, dt, side="left"):
        return stack.enter_context(nc.sbuf_tensor(name, shape, dt, side=side))

    onb = sbt(es, "onb", [128, 128], BF16)
    idb = sbt(es, "idb", [128, 128], BF16)
    onr = sbt(es, "onr", [1, 128], F32)
    gateb = sbt(es, "gateb", [128, NCH, NE], BF16)
    bqc = sbt(es, "bqc", [128, 8], F32)
    bvc = sbt(es, "bvc", [128, 8], F32)
    boc = sbt(es, "boc", [128, 32], F32)
    gamc = sbt(es, "gamc", [128, 32], F32)
    betc = sbt(es, "betc", [128, 32], F32)
    m_rep = sbt(es, "m_rep", [128, T], F32)
    r_rep = sbt(es, "r_rep", [128, T], F32)
    t_rep = sbt(es, "t_rep", [128, T], F32)
    s_rep = sbt(es, "s_rep", [128, T], F32)
    rcols = sbt(es, "rcols", [128, 4], F32)
    ncol = sbt(es, "ncol", [128, 1], F32)
    rsum = sbt(es, "rsum", [128, 1], F32)
    rrec = sbt(es, "rrec", [128, 1], F32)
    sqr = sbt(es, "sqr", [128, 2, T], BF16)
    EW = 4
    Ltok = sbt(es, "Ltok", [128, 4, NE], F32)
    L2 = sbt(es, "L2", [128, 4, 2 * NE], F32)
    Ltd = sbt(es, "Ltd", [128, 4, NE], F32)
    Rmax = sbt(es, "Rmax", [128, 4, NE - 1], F32)
    s_cols = sbt(es, "s_cols", [128, 4], F32)
    s_row = sbt(es, "s_row", [1, T], F32)
    epsc = sbt(es, "epsc", [128, 1], F32)

    sgx = Sem(es, nc, "sgx")
    sxm = Sem(es, nc, "sxm")
    sxr = Sem(es, nc, "sxr")
    sxl = [Sem(es, nc, "sxl0"), Sem(es, nc, "sxl1")]
    sxs = [Sem(es, nc, "sxs0"), Sem(es, nc, "sxs1")]
    SXW = {}
    for rname in ("rA", "rB", "rF", "rX", "rD"):
        SXW[rname] = [Sem(es, nc, f"sxw_{rname}{i}") for i in range(4)]
    st_ = Sem(es, nc, "st")
    stm = Sem(es, nc, "stm")
    sv = Sem(es, nc, "sv")
    ss = Sem(es, nc, "ss")

    ENG = {"v": "vector", "s": "scalar", "g": "gpsimd"}
    SEMS = {"v": sv, "s": ss, "g": sgx}

    psp = es.enter_context(nc.psum_tensor("psp", [128, 8, T], F32))
    ps_a = psp[:, 0:4]
    ps_a_flat = psp[:, 0:4].rearrange("p a b -> p (a b)")
    ps_b = psp[:, 0:2]
    ps_c = psp[:, 6:8]
    ps_pt = [psp[:, 4].bitcast(BF16).rearrange("p (a b) -> p a b", b=128),
             psp[:, 5].bitcast(BF16).rearrange("p (a b) -> p a b", b=128)]
    ps_sm = psp[:, 0, 0:32].rearrange("p (a b) -> p a b", b=NE)
    ps_srep = psp[:, 1]

    xTr = xT[:, :].rearrange("(c p) s -> p c s", p=128)

    def wstream(ring, rname, srcs, lastg_vals, lastg2_vals, scr=None):
        sxw = SXW[rname]
        n = len(srcs)
        nsl = ring.nslots
        i0 = ring.nload
        w0 = [s.n for s in sxw]

        def stores_before(i, slot):
            return sum(1 for j in range(i + 1) if (i0 + j) % nsl == slot)

        def pre(g):
            for j in range(min(nsl, n)):
                sl = (i0 + j) % nsl
                ring.load(g, srcs[j], free=[(sxw[sl], w0[sl])])

        def rest(g):
            for j in range(min(nsl, n), n):
                sl = (i0 + j) % nsl
                fr = [(st_, lastg_vals[j - nsl]),
                      (stm, lastg2_vals[j - nsl])]
                if scr is not None:
                    fr.append((sxw[sl], w0[sl] + 16 * stores_before(j - nsl, sl)))
                else:
                    fr.append((sxw[sl], w0[sl]))
                ring.load(g, srcs[j], free=fr)

        if scr is None:
            return pre, rest, None

        def syncf(s):
            for j in range(n):
                sl = (i0 + j) % nsl
                sem, val = ring.wait(i0 + j)
                s.wait_ge(sem.h, val)
                sxw[sl].inc(s.dma_start(scr[j], ring.buf[:, sl]), 16)
        return pre, rest, syncf

    def split_loads(ring, srcs, free_vals=None):
        npre = ring.nslots

        def pre(g):
            for sap in srcs[:npre]:
                ring.load(g, sap)

        def rest(g):
            for i in range(npre, len(srcs)):
                ring.load(g, srcs[i],
                          free=(free_vals[i] if free_vals else None))
        return pre, rest

    def block(**bodies):
        with nc.Block() as blk:
            for name, fn in bodies.items():
                getattr(blk, name)(fn)

    class _TRec:
        def __init__(self, t):
            self.t = t
            self.last = None

        def _push(self, ins):
            if self.last is not None:
                stm.inc(self.last)
            self.last = ins
            return ins

        def matmul(self, *a, **k):
            return self._push(self.t.matmul(*a, **k))

        def transpose(self, *a, **k):
            return self._push(self.t.transpose(*a, **k))

        def wait_ge(self, *a, **k):
            return self.t.wait_ge(*a, **k)

    def mm_stream(groups, gpsimd_fn=None, sync_fn=None):
        t_base = st_.n
        stm0 = stm.n
        cnt = {k: SEMS[k].n for k in SEMS}
        stage_vals, bank_free, stm_after = [], [], []
        acc = stm0
        for g in groups:
            vals = []
            for (ek, fn, _w) in g["evac"]:
                cnt[ek] += 1
                vals.append((ek, cnt[ek]))
            stage_vals.append(vals)
            bi = g.get("bank_stage")
            if bi is None:
                bi = len(vals) - 1
            bank_free.append(vals[bi] if vals else None)
            acc += max(0, g.get("nmm", 32) - 1)
            stm_after.append(acc)

        def resolve(w):
            out = []
            for item in (w or []):
                if isinstance(item[0], str) and item[0] == "stage":
                    _, gj, si = item
                    ek, val = stage_vals[gj][si]
                    out.append((SEMS[ek], val))
                elif isinstance(item[0], str) and item[0] == "stm_through":
                    out.append((stm, stm_after[item[1]]))
                elif isinstance(item[0], str) and item[0] == "st_done":
                    out.append((st_, t_base + item[1] + 1))
                else:
                    out.append(item)
            return out

        last_by_key = {}
        with nc.Block() as blk:
            @blk.tensor
            def _(t):
                for j, g in enumerate(groups):
                    for (sem, val) in resolve(g.get("waits")):
                        t.wait_ge(sem.h, val)
                    key, nsl = g.get("bank", ("c", 2))
                    hist = last_by_key.setdefault(key, [])
                    if len(hist) >= nsl:
                        jj = hist[-nsl]
                        if bank_free[jj] is not None:
                            ek, val = bank_free[jj]
                            t.wait_ge(SEMS[ek].h, val)
                    hist.append(j)
                    rec = _TRec(t)
                    g["mms"](rec)
                    st_.inc(rec.last)

            for ekey in ("v", "s", "g"):
                stages = [(j, si, fn, w) for j, g in enumerate(groups)
                          for si, (ek, fn, w) in enumerate(g["evac"])
                          if ek == ekey]
                if not stages:
                    continue

                def body(eng, stages=stages, ekey=ekey):
                    for (j, si, fn, w) in stages:
                        if si == 0:
                            eng.wait_ge(st_.h, t_base + j + 1)
                        else:
                            pk, pv = stage_vals[j][si - 1]
                            eng.wait_ge(SEMS[pk].h, pv)
                        for (sem, val) in resolve(w):
                            eng.wait_ge(sem.h, val)
                        SEMS[ekey].inc(fn(eng))
                getattr(blk, ENG[ekey])(body)

            if gpsimd_fn is not None:
                blk.gpsimd(gpsimd_fn)
            if sync_fn is not None:
                blk.sync(sync_fn)

    def stats_block(src_chunks, with_sq=True, xb_wait=None):
        n = len(src_chunks)
        sv0, st0 = sv.n, st_.n

        def tfn(t):
            for (sem, val) in (xb_wait or []):
                t.wait_ge(sem.h, val)
            for c, ap in enumerate(src_chunks):
                t.matmul(ps_b[:, 0], onb[:], ap, start=(c == 0), stop=(c == n - 1))
            if with_sq:
                for c in range(n):
                    t.wait_ge(sv.h, sv0 + c + 1)
                    st_.inc(t.matmul(ps_b[:, 1], onb[:], sqr[:, c % 2],
                                     start=(c == 0), stop=(c == n - 1)))

        def vfn(v):
            for (sem, val) in (xb_wait or []):
                v.wait_ge(sem.h, val)
            for c, ap in enumerate(src_chunks):
                if c >= 2:
                    v.wait_ge(st_.h, st0 + (c - 1))
                sv.inc(v.tensor_mul(sqr[:, c % 2], ap, ap))
        if with_sq:
            block(tensor=tfn, vector=vfn)
        else:
            block(tensor=tfn)

    def ln_finish(with_var=True):
        def vfn(v):
            v.tensor_scalar_mul(m_rep[:], ps_b[:, 0], 1.0 / H)
            if with_var:
                v.drain()
                v.tensor_mul(t_rep[:], m_rep[:], m_rep[:])
                v.drain()
                v.scalar_tensor_tensor(t_rep[:], ps_b[:, 1], 1.0 / H,
                                       t_rep[:], ALU.mult, ALU.subtract)
        block(vector=vfn)
        if with_var:
            block(scalar=lambda s: s.activation(t_rep[:], t_rep[:], AF.Sqrt,
                                                bias=epsc[:]))
            block(vector=lambda v: v.reciprocal(r_rep[:], t_rep[:]))

    def p0_sync(s):
        for dst, src in [(onb, ones_bf), (idb, ident_bf), (onr, onesrow),
                         (bqc, bq_c), (bvc, bv_c), (boc, bo_c),
                         (gamc, gam_c), (betc, bet_c), (epsc, eps_in)]:
            sxm.inc(s.dma_start(dst[:], src[:]), 16)

    def p0_g(g):
        sgx.inc(g.dma_start(gateb[:], Wg[:]), 16)
    block(sync=p0_sync, gpsimd=p0_g)
    c_setup_sx = sxm.n

    Kb = sbt(kv_es, "Kb", [128, 8, S], BF16)
    Vb = sbt(kv_es, "Vb", [128, 16, D], BF16)
    qb = sbt(kv_es, "qb", [128, 8, TOK], BF16)
    C1a = sbt(p1_es, "C1a", [128, NCH, T], BF16, side="right")
    C2a = sbt(p1_es, "C2a", [128, NCH, T], BF16, side="right")
    wr2a = sbt(p1_es, "wr2a", [128, 3, NCH, 128], BF16, side="right")
    rA = Ring(es, nc, "rA", wr2a, 3)

    for stt in range(4):
        t0 = stt * T
        own = stt < 2
        kc0 = stt * 4
        i_w0 = rA.nload

        nq = 8 if own else 0
        if stt == 0:
            srcs = [Wqkv[j] for j in range(24)]
            scr = [qkv_scr[j] for j in range(24)]
        else:
            srcs = ([qkv_scr[dc] for dc in range(8)] if own else []) + \
                   [qkv_scr[8 + dc] for dc in range(8)] + \
                   [qkv_scr[16 + dc] for dc in range(8)]
            scr = None
        lgv, lgv2 = [], []
        pre_ld, rest_ld, sync_wb = wstream(rA, "rA", srcs, lgv, lgv2, scr=scr)

        if stt == 0:
            def ld(g, pre_ld=pre_ld):
                sgx.inc(g.dma_start(C1a[:], xTr[:, :, 0:T]), 16)
                pre_ld(g)
            block(gpsimd=ld)
            c_xb = sgx.n
        else:
            block(gpsimd=pre_ld)
            c_xb = c_xb_next

        stats_block([C1a[:, c] for c in range(NCH)], with_sq=True,
                    xb_wait=[(sgx, c_xb)] + ([(sxm, c_setup_sx)] if stt == 0 else []))
        ln_finish(with_var=True)

        c_rrow = sxr.n + 16
        c_rcols = sxr.n + 32

        c_xcv = sv.n + 1

        def xcv(v):
            sv.inc(v.tensor_sub(C2a[:], C1a[:],
                                m_rep[:].unsqueeze(1).broadcast_to([128, NCH, T])))

        def rsync(s):
            sxr.inc(s.dma_start(r_scr[:], r_rep[0:1, :]), 16)
            s.wait_ge(sxr.h, c_rrow)
            sxr.inc(s.dma_start(rcols[:],
                                r_scr[:].rearrange("(t p) -> p t", p=128),
                                allow_slow_non_contiguous=True), 16)
        block(vector=xcv, sync=rsync)

        groups = []
        jq = 0
        BK = [2, 3, 4, 5, 6, 7]
        if own:
            for dc in range(8):
                j = jq
                bk = BK[j % 6]

                def mk(t, j=j, bk=bk):
                    sl = (i_w0 + j) % 3
                    for hc in range(NCH):
                        t.matmul(psp[:, bk], wr2a[:, sl, hc], C2a[:, hc],
                                 start=(hc == 0), stop=(hc == NCH - 1))

                def ev(v, dc=dc, bk=bk, t0=t0):
                    return v.tensor_mul(qb[:, dc, t0:t0 + T], psp[:, bk],
                                        r_rep[:])
                groups.append({"mms": mk, "evac": [("v", ev, None)],
                               "waits": [rA.wait(i_w0 + j)], "nmm": 32,
                               "bank": ("c", 6)})
                jq += 1
        for dc in range(8):
            j = jq
            bk = BK[j % 6]

            def mk(t, j=j, bk=bk):
                sl = (i_w0 + j) % 3
                for hc in range(NCH):
                    t.matmul(psp[:, bk], wr2a[:, sl, hc], C2a[:, hc],
                             start=(hc == 0), stop=(hc == NCH - 1))

            def ev(v, dc=dc, bk=bk, t0=t0):
                return v.tensor_mul(Kb[:, dc, t0:t0 + T], psp[:, bk],
                                    r_rep[:])
            groups.append({"mms": mk, "evac": [("v", ev, None)],
                           "waits": [rA.wait(i_w0 + j)], "nmm": 32,
                           "bank": ("c", 6)})
            jq += 1
        for dc in range(8):
            vblk = nq + 8 + dc
            for tc in range(4):
                j = jq
                bk = BK[j % 6]

                def mk(t, vblk=vblk, tc=tc, bk=bk):
                    sl = (i_w0 + vblk) % 3
                    for hc in range(NCH):
                        t.matmul(psp[:, bk, 0:128],
                                 C2a[:, hc, tc * 128:(tc + 1) * 128],
                                 wr2a[:, sl, hc],
                                 start=(hc == 0), stop=(hc == NCH - 1))

                def ev(s, dc=dc, tc=tc, bk=bk, kc0=kc0):
                    return s.activation(
                        Vb[:, kc0 + tc, dc * 128:(dc + 1) * 128],
                        psp[:, bk, 0:128], AF.Identity,
                        scale=rcols[:, tc:tc + 1])
                groups.append({"mms": mk, "evac": [("s", ev, [(sxr, c_rcols)])],
                               "waits": [rA.wait(i_w0 + vblk)], "nmm": 32,
                               "bank": ("c", 6)})
                jq += 1

        def lastg(b, nq=nq):
            if b < nq + 8:
                return b
            return nq + 8 + (b - nq - 8) * 4 + 3
        t_base0 = st_.n
        stm0_ = stm.n
        _nmm = [g.get("nmm", 32) for g in groups]
        _cum = [0]
        for v in _nmm:
            _cum.append(_cum[-1] + max(0, v - 1))
        lgv[:] = [t_base0 + lastg(b) + 1 for b in range(len(srcs))]
        lgv2[:] = [stm0_ + _cum[lastg(b) + 1] for b in range(len(srcs))]

        if stt < 3:
            c_xb_next = sgx.n + 16

            def gfn(g, stt=stt, c_xcv=c_xcv, rest_ld=rest_ld):
                g.wait_ge(sv.h, c_xcv)
                sgx.inc(g.dma_start(C1a[:],
                                    xTr[:, :, (stt + 1) * T:(stt + 2) * T]), 16)
                rest_ld(g)
        else:
            def gfn(g, rest_ld=rest_ld):
                rest_ld(g)
        mm_stream(groups, gpsimd_fn=gfn, sync_fn=sync_wb)

    def qbias(s):
        for dc in range(8):
            s.activation(qb[:, dc], qb[:, dc], AF.Identity, bias=bqc[:, dc:dc + 1])
            s.drain()
    block(scalar=qbias)

    def _q1(s):
        for i in range(3):
            s.wait_ge(SXW["rA"][i].h, SXW["rA"][i].n)
    block(sync=_q1)
    p1_es.close()

    if stage == "pass1":
        def taps(s):
            if "K" in dbg_t:
                for dc in range(8):
                    sxm.inc(s.dma_start(dbg_t["K"][dc], Kb[:, dc]), 16)
            if "V" in dbg_t:
                for kc in range(16):
                    sxm.inc(s.dma_start(dbg_t["V"][kc], Vb[:, kc]), 16)
            if "q" in dbg_t:
                for dc in range(8):
                    sxm.inc(s.dma_start(dbg_t["q"][dc], qb[:, dc]), 16)
            s.wait_ge(sxm.h, sxm.n)
        block(sync=taps)
        for stk in (att_es, ffn_es, p2_es, kv_es, es):
            stk.close()
        return nc, dbg_t

    Ab = sbt(p2_es, "Ab", [128, NCH, T], BF16, side="right")
    xf32 = sbt(p2_es, "xf32", [128, 2, EW, T], F32, side="right")
    yf32 = sbt(p2_es, "yf32", [128, 2, EW, T], F32, side="right")
    wo2 = sbt(att_es, "wo2", [128, 2, 8, 128], BF16, side="right")
    Pb = sbt(att_es, "Pb", [128, 2048], BF16, side="right")
    PTb = sbt(att_es, "PTb", [128, 16, T], BF16, side="right")
    aob = sbt(att_es, "aob", [128, 8, T], BF16, side="right")
    rB = Ring(es, nc, "rB", wo2, 2)

    for ot in range(2):
        tq0 = ot * T
        i_o0 = rB.nload

        if ot == 0:
            osrcs = [Wo[hc] for hc in range(NCH)]
            oscr = [wo_scr[hc] for hc in range(NCH)]
        else:
            osrcs = [wo_scr[hc] for hc in range(NCH)]
            oscr = None
        olgv, olgv2 = [], []
        pre_o, rest_o, sync_owb = wstream(rB, "rB", osrcs, olgv, olgv2, scr=oscr)
        block(gpsimd=pre_o)

        groups = []
        gidx = {}
        s_defs = {}
        t_defs = {}
        for qs in range(4):
            def mk_s(t, qs=qs, tq0=tq0):
                for dc in range(8):
                    for kg in range(4):
                        t.matmul(ps_a[:, kg],
                                 qb[:, dc, tq0 + qs * 128:tq0 + (qs + 1) * 128],
                                 Kb[:, dc, kg * T:(kg + 1) * T],
                                 start=(dc == 0), stop=(dc == 7))
            s_defs[qs] = {
                "mms": mk_s,
                "evac": [
                    ("v", lambda v: v.reduce_max(ncol[:], ps_a_flat,
                                                 axis=AX.X, negate=True), None),
                    ("v", lambda v: v.tensor_scalar_mul(ncol[:], ncol[:],
                                                        1.0 / 32.0), None),
                    ("s", lambda s: s.activation(Pb[:], ps_a_flat, AF.Exp,
                                                 bias=ncol[:], scale=1.0 / 32.0),
                     ([("stm_through", ("T", qs - 1, 1)),
                       ("st_done", ("T", qs - 1, 1))]
                      if qs >= 1 else None)),
                    ("v", lambda v: v.reduce_sum(rsum[:], Pb[:], axis=AX.X),
                     None),
                    ("v", lambda v: v.reciprocal(rrec[:], rsum[:]), None),
                    ("v", lambda v: v.tensor_scalar_mul(Pb[:], Pb[:], rrec[:]),
                     None),
                ],
                "waits": None, "bank": ("a", 1), "bank_stage": 2, "nmm": 32,
            }
            for g2 in range(2):
                def mk_t(t, g2=g2):
                    for jj in range(8):
                        kc = g2 * 8 + jj
                        t.transpose(ps_pt[g2][:, jj],
                                    Pb[:, kc * 128:(kc + 1) * 128], idb[:])

                def ev_pt(v, qs=qs, g2=g2):
                    return v.tensor_copy(
                        PTb[:, g2 * 8:(g2 + 1) * 8, qs * 128:(qs + 1) * 128],
                        ps_pt[g2][:])
                t_defs[(qs, g2)] = {"mms": mk_t, "evac": [("v", ev_pt, None)],
                                    "waits": [("stage", ("S", qs), 5)],
                                    "bank": ("pt", 2), "nmm": 8}
        order = [("S", 0), ("S", 1), ("T", 0, 0), ("T", 0, 1), ("S", 2),
                 ("T", 1, 0), ("T", 1, 1), ("S", 3), ("T", 2, 0), ("T", 2, 1),
                 ("T", 3, 0), ("T", 3, 1)]
        for key in order:
            gidx[key] = len(groups)
            groups.append(s_defs[key[1]] if key[0] == "S" else t_defs[key[1:]])
        for g in groups:
            w = g.get("waits")
            if w:
                g["waits"] = [(("stage", gidx[x[1]], x[2]) if (x[0] == "stage" and isinstance(x[1], tuple)) else x)
                              for x in w]
            ev = []
            for (ek, fn, xw) in g["evac"]:
                if xw:
                    xw = [((y[0], gidx[y[1]]) if (y[0] in ("stm_through", "st_done") and isinstance(y[1], tuple)) else y)
                          for y in xw]
                ev.append((ek, fn, xw))
            g["evac"] = ev

        BKA = [6, 7, 0, 1, 2, 3]
        for dc in range(8):
            bk = BKA[dc % 6]

            def mk_pv(t, dc=dc, bk=bk):
                for kc in range(16):
                    t.matmul(psp[:, bk],
                             Vb[:, kc, dc * 128:(dc + 1) * 128],
                             PTb[:, kc], start=(kc == 0), stop=(kc == 15))

            def ev_pv(s, dc=dc, bk=bk):
                return s.activation(aob[:, dc], psp[:, bk], AF.Identity,
                                    bias=bvc[:, dc:dc + 1])
            gidx[("PV", dc)] = len(groups)
            w = []
            if dc == 0:
                w.append(("stage", gidx[("T", 3, 1)], 0))
            if dc == 2:
                w.append(("stage", gidx[("S", 3)], 2))
            groups.append({
                "mms": mk_pv, "evac": [("s", ev_pv, None)],
                "waits": w or None,
                "bank": ("c", 6), "nmm": 16})
        for hc in range(NCH):
            bk = BKA[(8 + hc) % 6]

            def mk_wo(t, hc=hc, bk=bk):
                sl = (i_o0 + hc) % 2
                for dc in range(8):
                    t.matmul(psp[:, bk], wo2[:, sl, dc], aob[:, dc],
                             start=(dc == 0), stop=(dc == 7))

            def ev_wo(s, hc=hc, bk=bk):
                return s.activation(Ab[:, hc], psp[:, bk], AF.Identity,
                                    bias=boc[:, hc:hc + 1])
            w = [rB.wait(i_o0 + hc)]
            if hc == 0:
                w.append(("stage", gidx[("PV", 7)], 0))
            groups.append({"mms": mk_wo, "evac": [("s", ev_wo, None)],
                           "waits": w, "bank": ("c", 6), "nmm": 8})
        t_base0 = st_.n
        stm0_ = stm.n
        g_wo0 = len(groups) - NCH
        _nmm = [g.get("nmm", 32) for g in groups]
        _cum = [0]
        for v in _nmm:
            _cum.append(_cum[-1] + max(0, v - 1))
        olgv[:] = [t_base0 + g_wo0 + b + 1 for b in range(NCH)]
        olgv2[:] = [stm0_ + _cum[g_wo0 + b + 1] for b in range(NCH)]
        mm_stream(groups, gpsimd_fn=rest_o, sync_fn=sync_owb)

        if "ao" in dbg_t or "o" in dbg_t:
            def tap_ao(s, ot=ot):
                if "ao" in dbg_t:
                    for dc in range(8):
                        sxm.inc(s.dma_start(dbg_t["ao"][ot, dc], aob[:, dc]), 16)
                if "o" in dbg_t:
                    for hc in range(NCH):
                        sxm.inc(s.dma_start(dbg_t["o"][ot, hc], Ab[:, hc]), 16)
                s.wait_ge(sxm.h, sxm.n)
            block(sync=tap_ao)

        stats_block([Ab[:, c] for c in range(NCH)], with_sq=True)
        ln_finish(with_var=True)

        sv0 = sv.n
        cl0 = [sxl[0].n, sxl[1].n]
        cs0 = [sxs[0].n, sxs[1].n]
        m2b = m_rep[:].unsqueeze(1).broadcast_to([128, EW, T])
        r2b = r_rep[:].unsqueeze(1).broadcast_to([128, EW, T])

        def ysync(s, ot=ot, tq0=tq0, sv0=sv0):
            for k in range(NCH // EW):
                if k >= 2:
                    s.wait_ge(sv.h, sv0 + (k - 1))
                sxl[k % 2].inc(s.dma_start(
                    xf32[:, k % 2],
                    xTr[:, EW * k:EW * (k + 1), tq0:tq0 + T]), 16)
                s.wait_ge(sv.h, sv0 + k + 1)
                sxs[k % 2].inc(s.dma_start(
                    y_scr[ot, EW * k:EW * (k + 1)].transpose([1, 0, 2]),
                    yf32[:, k % 2]), 16)

        def yvec(v, sv0=sv0, cl0=cl0, cs0=cs0):
            for k in range(NCH // EW):
                v.wait_ge(sxs[k % 2].h, cs0[k % 2] + 16 * (k // 2))
                v.tensor_sub(yf32[:, k % 2], Ab[:, EW * k:EW * (k + 1)], m2b)
                v.drain()
                v.tensor_mul(yf32[:, k % 2], yf32[:, k % 2], r2b)
                v.drain()
                v.wait_ge(sxl[k % 2].h, cl0[k % 2] + 16 * (k // 2 + 1))
                sv.inc(v.tensor_add(yf32[:, k % 2], yf32[:, k % 2],
                                    xf32[:, k % 2]))
        block(sync=ysync, vector=yvec)

        if "y" in dbg_t:
            def tap_y(s, ot=ot):
                s.wait_ge(sxs[0].h, sxs[0].n)
                s.wait_ge(sxs[1].h, sxs[1].n)
                for hc in range(NCH):
                    sxm.inc(s.dma_start(dbg_t["y"][ot, hc], y_scr[ot, hc]), 16)
                s.wait_ge(sxm.h, sxm.n)
            block(sync=tap_y)

    def _q2(s):
        for i in range(2):
            s.wait_ge(SXW["rB"][i].h, SXW["rB"][i].n)
    block(sync=_q2)
    att_es.close()
    kv_es.close()

    if stage == "attn":
        def _fw(s):
            s.wait_ge(sxs[0].h, sxs[0].n)
            s.wait_ge(sxs[1].h, sxs[1].n)
        block(sync=_fw)
        for stk in (ffn_es, p2_es, es):
            stk.close()
        return nc, dbg_t

    C1f = sbt(ffn_es, "C1f", [128, NCH, T], BF16, side="right")
    C2f = sbt(ffn_es, "C2f", [128, NCH, T], BF16, side="right")
    wrf = sbt(ffn_es, "wrf", [128, 3, NCH, 128], BF16, side="right")
    wxf = sbt(ffn_es, "wxf", [128, 2, 16, 128], BF16, side="right")
    rF = Ring(es, nc, "rF", wrf, 3)
    rX = Ring(es, nc, "rX", wxf, 2)

    def hchunk(ic):
        return C1f[:, ic] if ic < NCH else Ab[:, ic - NCH]

    for ot in range(2):
        tq0 = ot * T
        lm_es = contextlib.ExitStack()
        latb = sbt(lm_es, f"latb{ot}", [128, 16, T], BF16, side="right")
        moeb = sbt(lm_es, f"moeb{ot}", [128, 16, T], BF16, side="right")

        c_ybf = sgx.n + 16

        def ldy(g, ot=ot):
            g.wait_ge(sxs[0].h, sxs[0].n)
            g.wait_ge(sxs[1].h, sxs[1].n)
            sgx.inc(g.dma_start(C1f[:], y_scr[ot].rearrange("c p t -> p c t")), 16)
        block(gpsimd=ldy)

        stats_block([C1f[:, c] for c in range(NCH)], with_sq=False,
                    xb_wait=[(sgx, c_ybf)])
        ln_finish(with_var=False)

        def xc3v(v):
            v.tensor_sub(C2f[:], C1f[:],
                         m_rep[:].unsqueeze(1).broadcast_to([128, NCH, T]))
        block(vector=xc3v)

        if "xc3" in dbg_t and ot == 0:
            def tapx(s):
                for hc in range(NCH):
                    sxm.inc(s.dma_start(dbg_t["xc3"][hc], C2f[:, hc]), 16)
                s.wait_ge(sxm.h, sxm.n)
            block(sync=tapx)

        def mk_gate(t):
            for tc in range(4):
                for hc in range(NCH):
                    t.matmul(ps_sm[:, tc], C2f[:, hc, tc * 128:(tc + 1) * 128],
                             gateb[:, hc], start=(hc == 0), stop=(hc == NCH - 1))
        mm_stream([{"mms": mk_gate,
                    "evac": [("v", lambda v: v.tensor_copy(Ltok[:], ps_sm[:]),
                              None)],
                    "waits": None, "bank": ("a", 1), "nmm": 128}])

        def top2(v):
            v.tensor_copy(L2[:, :, 0:NE], Ltok[:])
            v.tensor_copy(L2[:, :, NE:2 * NE], Ltok[:])
            v.drain()
            for d in range(1, NE):
                v.tensor_add(Ltd[:], L2[:, :, 0:NE], L2[:, :, d:d + NE])
                v.drain()
                v.reduce_max(Rmax[:, :, d - 1:d], Ltd[:], axis=AX.X)
                v.drain()
            v.reduce_max(s_cols[:].unsqueeze(2), Rmax[:], axis=AX.X)
        block(vector=top2)

        c_scols = sxr.n + 16
        c_srow = sxr.n + 32

        def s_sync(s):
            sxr.inc(s.dma_start(s_scr[:].rearrange("(t p) -> p t", p=128),
                                s_cols[:], allow_slow_non_contiguous=True), 16)
            s.wait_ge(sxr.h, c_scols)
            sxr.inc(s.dma_start(s_row[:], s_scr[:].unsqueeze(0)), 16)
        block(sync=s_sync)

        def mk_srep(t):
            t.wait_ge(sxr.h, c_srow)
            t.matmul(ps_srep[:], onr[:], s_row[:], start=True, stop=True)
        mm_stream([{"mms": mk_srep,
                    "evac": [("v", lambda v: v.tensor_copy(s_rep[:], ps_srep[:]),
                              None)],
                    "waits": None, "bank": ("b", 1), "nmm": 1}])

        if "srep" in dbg_t:
            def taps2(s, ot=ot):
                sxm.inc(s.dma_start(dbg_t["srep"][ot], s_rep[:]), 16)
                s.wait_ge(sxm.h, sxm.n)
            block(sync=taps2)

        i_f0 = rF.nload

        f1srcs = [Wf1[lc] for lc in range(16)] if ot == 0 else \
                 [f1_scr[lc] for lc in range(16)]
        f1lgv, f1lgv2 = [], []
        pre_f1, rest_f1, sync_f1wb = wstream(
            rF, "rF", f1srcs, f1lgv, f1lgv2,
            scr=([f1_scr[lc] for lc in range(16)] if ot == 0 else None))
        block(gpsimd=pre_f1)

        groups = []
        BK = [2, 3, 4, 5, 6, 7]
        for lc in range(16):
            bk = BK[lc % 6]

            def mk(t, lc=lc, bk=bk):
                sl = (i_f0 + lc) % 3
                for hc in range(NCH):
                    t.matmul(psp[:, bk], wrf[:, sl, hc], C2f[:, hc],
                             start=(hc == 0), stop=(hc == NCH - 1))

            def ev(v, lc=lc, bk=bk):
                return v.tensor_copy(latb[:, lc], psp[:, bk])
            groups.append({"mms": mk, "evac": [("v", ev, None)],
                           "waits": [rF.wait(i_f0 + lc)], "nmm": 32,
                           "bank": ("c", 6)})
        t_base0 = st_.n
        f1lgv[:] = [t_base0 + b + 1 for b in range(16)]
        f1lgv2[:] = [stm.n + 31 * (b + 1) for b in range(16)]
        mm_stream(groups, gpsimd_fn=rest_f1, sync_fn=sync_f1wb)

        i_x0 = rX.nload

        exsrcs = [Wex[mc] for mc in range(16)] if ot == 0 else \
                 [ex_scr[mc] for mc in range(16)]
        exlgv, exlgv2 = [], []
        pre_ex, rest_ex, sync_exwb = wstream(
            rX, "rX", exsrcs, exlgv, exlgv2,
            scr=([ex_scr[mc] for mc in range(16)] if ot == 0 else None))
        block(gpsimd=pre_ex)

        groups = []
        for mc in range(16):
            bk = BK[mc % 6]

            def mk(t, mc=mc, bk=bk):
                sl = (i_x0 + mc) % 2
                for lc in range(16):
                    t.matmul(psp[:, bk], wxf[:, sl, lc], latb[:, lc],
                             start=(lc == 0), stop=(lc == 15))

            def ev(v, mc=mc, bk=bk):
                return v.tensor_mul(moeb[:, mc], psp[:, bk], s_rep[:])
            groups.append({"mms": mk, "evac": [("v", ev, None)],
                           "waits": [rX.wait(i_x0 + mc)], "nmm": 16,
                           "bank": ("c", 6)})
        t_base0 = st_.n
        exlgv[:] = [t_base0 + b + 1 for b in range(16)]
        exlgv2[:] = [stm.n + 15 * (b + 1) for b in range(16)]
        mm_stream(groups, gpsimd_fn=rest_ex, sync_fn=sync_exwb)

        if "lat" in dbg_t and ot == 0:
            def tapl(s):
                for lc in range(16):
                    sxm.inc(s.dma_start(dbg_t["lat"][lc], latb[:, lc]), 16)
                    sxm.inc(s.dma_start(dbg_t["moe"][lc], moeb[:, lc]), 16)
                s.wait_ge(sxm.h, sxm.n)
            block(sync=tapl)

        i_u0 = rF.nload

        upsrcs = [Wup[ic] for ic in range(64)] if ot == 0 else \
                 [up_scr[ic] for ic in range(64)]
        uplgv, uplgv2 = [], []
        pre_up, rest_up, sync_upwb = wstream(
            rF, "rF", upsrcs, uplgv, uplgv2,
            scr=([up_scr[ic] for ic in range(64)] if ot == 0 else None))
        block(gpsimd=pre_up)

        groups = []
        for ic in range(64):
            bk = BK[ic % 6]

            def mk(t, ic=ic, bk=bk):
                sl = (i_u0 + ic) % 3
                for hc in range(NCH):
                    t.matmul(psp[:, bk], wrf[:, sl, hc], C2f[:, hc],
                             start=(hc == 0), stop=(hc == NCH - 1))

            def ev1(s, ic=ic, bk=bk):
                return s.activation(sqr[:, ic % 2], psp[:, bk], AF.Relu)

            def ev2(v, ic=ic):
                return v.tensor_mul(hchunk(ic), sqr[:, ic % 2], sqr[:, ic % 2])
            groups.append({"mms": mk,
                           "evac": [("s", ev1,
                                     ([("stage", ic - 2, 1)] if ic >= 2 else None)),
                                    ("v", ev2, None)],
                           "waits": [rF.wait(i_u0 + ic)], "nmm": 32,
                           "bank": ("c", 6), "bank_stage": 0})
        t_base0 = st_.n
        uplgv[:] = [t_base0 + b + 1 for b in range(64)]
        uplgv2[:] = [stm.n + 31 * (b + 1) for b in range(64)]
        mm_stream(groups, gpsimd_fn=rest_up, sync_fn=sync_upwb)

        if "hh" in dbg_t and ot == 0:
            def taph(s):
                for ic in range(16):
                    sxm.inc(s.dma_start(dbg_t["hh"][ic], hchunk(ic * 4)), 16)
                s.wait_ge(sxm.h, sxm.n)
            block(sync=taph)

        i_x1 = rX.nload

        f2srcs = [Wf2[oc] for oc in range(NCH)] if ot == 0 else \
                 [f2_scr[oc] for oc in range(NCH)]
        f2lgv, f2lgv2 = [], []
        pre_f2, rest_f2, sync_f2wb = wstream(
            rX, "rX", f2srcs, f2lgv, f2lgv2,
            scr=([f2_scr[oc] for oc in range(NCH)] if ot == 0 else None))
        block(gpsimd=pre_f2)

        groups = []
        for oc in range(NCH):
            bk = BK[oc % 6]

            def mk(t, oc=oc, bk=bk):
                sl = (i_x1 + oc) % 2
                for lc in range(16):
                    t.matmul(psp[:, bk], wxf[:, sl, lc], moeb[:, lc],
                             start=(lc == 0), stop=(lc == 15))

            def ev(v, oc=oc, bk=bk):
                return v.tensor_copy(C2f[:, oc], psp[:, bk])
            groups.append({"mms": mk, "evac": [("v", ev, None)],
                           "waits": [rX.wait(i_x1 + oc)], "nmm": 16,
                           "bank": ("c", 6)})
        t_base0 = st_.n
        f2lgv[:] = [t_base0 + b + 1 for b in range(NCH)]
        f2lgv2[:] = [stm.n + 15 * (b + 1) for b in range(NCH)]
        mm_stream(groups, gpsimd_fn=rest_f2, sync_fn=sync_f2wb)

        lm_es.close()
        dn_es = contextlib.ExitStack()
        wdf = sbt(dn_es, f"wdf{ot}", [128, 4, NCH, 128], BF16, side="right")
        rD = Ring(es, nc, f"rD{ot}", wdf, 4)
        SXW[f"rD{ot}"] = SXW["rD"]

        i_d0 = rD.nload

        dnsrcs = [(Wdn if ot == 0 else dn_scr)[oc * 2 + half]
                  for oc in range(NCH) for half in range(2)]
        dnlgv, dnlgv2 = [], []
        pre_dn, rest_dn, sync_dnwb = wstream(
            rD, f"rD{ot}", dnsrcs, dnlgv, dnlgv2,
            scr=([dn_scr[j] for j in range(64)] if ot == 0 else None))
        block(gpsimd=pre_dn)

        groups = []
        for oc in range(NCH):
            bk = BK[oc % 6]

            def mk(t, oc=oc, bk=bk):
                for half in range(2):
                    sl = (i_d0 + 2 * oc + half) % 4
                    for j in range(NCH):
                        t.matmul(psp[:, bk], wdf[:, sl, j],
                                 hchunk(half * NCH + j),
                                 start=(half == 0 and j == 0),
                                 stop=(half == 1 and j == NCH - 1))

            def ev(v, oc=oc, bk=bk):
                return v.tensor_add(C2f[:, oc], psp[:, bk], C2f[:, oc])
            groups.append({"mms": mk, "evac": [("v", ev, None)],
                           "waits": [rD.wait(i_d0 + 2 * oc), rD.wait(i_d0 + 2 * oc + 1)],
                           "bank": ("c", 6), "nmm": 64})
        t_base0 = st_.n
        dnlgv[:] = [t_base0 + b // 2 + 1 for b in range(64)]
        dnlgv2[:] = [stm.n + 63 * (b // 2 + 1) for b in range(64)]
        mm_stream(groups, gpsimd_fn=rest_dn, sync_fn=sync_dnwb)

        def _q3(s):
            for i in range(4):
                s.wait_ge(SXW["rD"][i].h, SXW["rD"][i].n)
        block(sync=_q3)
        dn_es.close()

        if "v" in dbg_t and ot == 0:
            def tapv(s):
                for oc in range(NCH):
                    sxm.inc(s.dma_start(dbg_t["v"][oc], C2f[:, oc]), 16)
                s.wait_ge(sxm.h, sxm.n)
            block(sync=tapv)

        stats_block([C2f[:, c] for c in range(NCH)], with_sq=True)
        ln_finish(with_var=True)

        sv0, ss0 = sv.n, ss.n
        m4b = m_rep[:].unsqueeze(1).broadcast_to([128, EW, T])
        r4b = r_rep[:].unsqueeze(1).broadcast_to([128, EW, T])

        def t4v(v, ss0=ss0):
            for k in range(NCH // EW):
                if k >= 2:
                    v.wait_ge(ss.h, ss0 + EW * (k - 1))
                v.tensor_sub(yf32[:, k % 2], C2f[:, EW * k:EW * (k + 1)], m4b)
                v.drain()
                sv.inc(v.tensor_mul(yf32[:, k % 2], yf32[:, k % 2], r4b))

        def t4s(s, sv0=sv0):
            for k in range(NCH // EW):
                s.wait_ge(sv.h, sv0 + k + 1)
                for j in range(EW):
                    oc = EW * k + j
                    ss.inc(s.activation(C1f[:, oc], yf32[:, k % 2, j],
                                        AF.Identity,
                                        scale=gamc[:, oc:oc + 1],
                                        bias=betc[:, oc:oc + 1]))
        block(vector=t4v, scalar=t4s)

        stats_block([C1f[:, c] for c in range(NCH)], with_sq=True)
        ln_finish(with_var=True)

        sv0 = sv.n
        cl0 = [sxl[0].n, sxl[1].n]
        cs0 = [sxs[0].n, sxs[1].n]
        m5b = m_rep[:].unsqueeze(1).broadcast_to([128, EW, T])
        r5b = r_rep[:].unsqueeze(1).broadcast_to([128, EW, T])

        def fsync(s, ot=ot, tq0=tq0, sv0=sv0):
            for k in range(NCH // EW):
                if k >= 2:
                    s.wait_ge(sv.h, sv0 + (k - 1))
                sxl[k % 2].inc(s.dma_start(
                    xf32[:, k % 2],
                    y_scr[ot, EW * k:EW * (k + 1)].transpose([1, 0, 2])), 16)
                s.wait_ge(sv.h, sv0 + k + 1)
                sxs[k % 2].inc(s.dma_start(
                    outT[EW * k * 128:EW * (k + 1) * 128, tq0:tq0 + T]
                    .rearrange("(c p) t -> p c t", p=128),
                    yf32[:, k % 2]), 16)

        def fvec(v, cl0=cl0, cs0=cs0):
            for k in range(NCH // EW):
                v.wait_ge(sxs[k % 2].h, cs0[k % 2] + 16 * (k // 2))
                v.tensor_sub(yf32[:, k % 2], C1f[:, EW * k:EW * (k + 1)], m5b)
                v.drain()
                v.tensor_mul(yf32[:, k % 2], yf32[:, k % 2], r5b)
                v.drain()
                v.wait_ge(sxl[k % 2].h, cl0[k % 2] + 16 * (k // 2 + 1))
                sv.inc(v.tensor_add(yf32[:, k % 2], yf32[:, k % 2],
                                    xf32[:, k % 2]))
        block(sync=fsync, vector=fvec)

    def _final_wait(s):
        s.wait_ge(sxs[0].h, sxs[0].n)
        s.wait_ge(sxs[1].h, sxs[1].n)
    block(sync=_final_wait)

    for stk in (ffn_es, p2_es, es):
        stk.close()
    return nc, dbg_t


_TW_CACHE = {}


def _tile4(W):
    W = np.asarray(W)
    K, N = W.shape
    return np.ascontiguousarray(
        W.reshape(K // 128, 128, N // 128, 128).transpose(2, 1, 0, 3))


def _tiled_weights(inp):
    key = id(inp.get("L0_attn_Wqkv"))
    if key in _TW_CACHE:
        return _TW_CACHE[key]
    dn = _tile4(inp["L0_ffn_down_w"])
    dn = np.ascontiguousarray(
        dn.reshape(32, 128, 2, 32, 128).transpose(0, 2, 1, 3, 4)
    ).reshape(64, 128, 32, 128)
    g = np.asarray(inp["L0_ffn_gate_w"])
    gt = np.ascontiguousarray(g.reshape(32, 128, 8).transpose(1, 0, 2))
    tw = {
        "Wqkv": _tile4(inp["L0_attn_Wqkv"]),
        "Wo": _tile4(inp["L0_attn_Wo"]),
        "Wup": _tile4(inp["L0_ffn_up_w"]),
        "Wdn": dn,
        "Wf1": _tile4(inp["L0_ffn_fc1_w"]),
        "Wf2": _tile4(inp["L0_ffn_fc2_w"]),
        "Wex": _tile4(inp["L0_ffn_expert_w"]),
        "Wg": gt,
    }
    _TW_CACHE.clear()
    _TW_CACHE[key] = tw
    return tw


def make_inputs(inp, core):
    b, half = core // 2, core % 2
    x = np.asarray(inp["L0_attn_x"])
    xb_ = x[b]
    rot = np.concatenate([xb_[half * TOK:(half + 1) * TOK],
                          xb_[(1 - half) * TOK:(2 - half) * TOK]], axis=0)
    bqkv = np.asarray(inp["L0_attn_bqkv"])
    tw = _tiled_weights(inp)
    return {
        "xT": np.ascontiguousarray(rot.T),
        **tw,
        "bq_c": np.ascontiguousarray(bqkv[0:D].reshape(8, 128).T),
        "bv_c": np.ascontiguousarray(bqkv[2 * D:3 * D].reshape(8, 128).T),
        "bo_c": np.ascontiguousarray(
            np.asarray(inp["L0_attn_bo"]).reshape(32, 128).T),
        "gam_c": np.ascontiguousarray(
            np.asarray(inp["L0_ffn_ln_gamma"]).reshape(32, 128).T),
        "bet_c": np.ascontiguousarray(
            np.asarray(inp["L0_ffn_ln_beta"]).reshape(32, 128).T),
        "ones_bf": np.ones((128, 128), bf16),
        "ident_bf": np.eye(128, dtype=bf16),
        "onesrow": np.ones((1, 128), np.float32),
        "eps_in": np.full((128, 1), EPS, np.float32),
    }


from concourse.bass_utils import run_bass_kernel_spmd

_PROG_CACHE = {}


def kernel(L0_attn_x, L0_attn_Wqkv, L0_attn_bqkv, L0_attn_Wo, L0_attn_bo,
           L0_ffn_gate_w, L0_ffn_up_w, L0_ffn_down_w, L0_ffn_fc1_w,
           L0_ffn_fc2_w, L0_ffn_expert_w, L0_ffn_ln_gamma, L0_ffn_ln_beta):
    inp = {
        "L0_attn_x": L0_attn_x, "L0_attn_Wqkv": L0_attn_Wqkv,
        "L0_attn_bqkv": L0_attn_bqkv, "L0_attn_Wo": L0_attn_Wo,
        "L0_attn_bo": L0_attn_bo, "L0_ffn_gate_w": L0_ffn_gate_w,
        "L0_ffn_up_w": L0_ffn_up_w, "L0_ffn_down_w": L0_ffn_down_w,
        "L0_ffn_fc1_w": L0_ffn_fc1_w, "L0_ffn_fc2_w": L0_ffn_fc2_w,
        "L0_ffn_expert_w": L0_ffn_expert_w,
        "L0_ffn_ln_gamma": L0_ffn_ln_gamma, "L0_ffn_ln_beta": L0_ffn_ln_beta,
    }
    if "nc" not in _PROG_CACHE:
        _PROG_CACHE["nc"] = build_program(stage="full")[0]
    nc = _PROG_CACHE["nc"]
    in_maps = [make_inputs(inp, c) for c in range(8)]
    res = run_bass_kernel_spmd(nc, in_maps, core_ids=list(range(8)))
    out = np.empty((B, S, H), np.float32)
    for c in range(8):
        b, half = c // 2, c % 2
        out[b, half * TOK:(half + 1) * TOK, :] = res.results[c]["outT"].T
    return out


# version 11
# speedup vs baseline: 1.0607x; 1.0607x over previous
import contextlib
import numpy as np
import ml_dtypes
import concourse.bass as bass
import concourse.mybir as mybir

F32 = mybir.dt.float32
BF16 = mybir.dt.bfloat16
AF = mybir.ActivationFunctionType
ALU = mybir.AluOpType
AX = mybir.AxisListType

B, S, H = 4, 2048, 4096
TOK = 1024
T = 512
NCH = H // 128
D = 1024
EPS = 1e-5
INT = 8192
LAT = 2048
NE = 8

bf16 = ml_dtypes.bfloat16


class Sem:
    def __init__(self, es, nc, name):
        self.h = es.enter_context(nc.semaphore(name))
        self.n = 0

    def inc(self, instr, k=1):
        instr.then_inc(self.h, k)
        self.n += k
        return self.n


class Ring:
    def __init__(self, es, nc, name, buf, nslots):
        self.buf, self.nslots = buf, nslots
        self.ls = [Sem(es, nc, f"{name}_l{i}") for i in range(nslots)]
        self.nload = 0

    def load(self, g, src_ap, free=()):
        i = self.nload
        for (sem, val) in free:
            g.wait_ge(sem.h, val)
        self.ls[i % self.nslots].inc(
            g.dma_start(self.buf[:, i % self.nslots], src_ap), 16)
        self.nload += 1

    def wait(self, i):
        return (self.ls[i % self.nslots], 16 * (i // self.nslots + 1))


def build_program(stage="full", dbg=()):
    nc = bass.Bass("TRN2", target_bir_lowering=False, debug=False)

    xT = nc.dram_tensor("xT", [H, S], F32, kind="ExternalInput")
    Wqkv = nc.dram_tensor("Wqkv", [24, 128, 32, 128], F32, kind="ExternalInput")
    Wo = nc.dram_tensor("Wo", [32, 128, 8, 128], F32, kind="ExternalInput")
    Wup = nc.dram_tensor("Wup", [64, 128, 32, 128], F32, kind="ExternalInput")
    Wdn = nc.dram_tensor("Wdn", [64, 128, 32, 128], F32, kind="ExternalInput")
    Wf1 = nc.dram_tensor("Wf1", [16, 128, 32, 128], F32, kind="ExternalInput")
    Wf2 = nc.dram_tensor("Wf2", [32, 128, 16, 128], F32, kind="ExternalInput")
    Wex = nc.dram_tensor("Wex", [16, 128, 16, 128], F32, kind="ExternalInput")
    Wg = nc.dram_tensor("Wg", [128, NCH, NE], F32, kind="ExternalInput")
    bq_c = nc.dram_tensor("bq_c", [128, 8], F32, kind="ExternalInput")
    bv_c = nc.dram_tensor("bv_c", [128, 8], F32, kind="ExternalInput")
    bo_c = nc.dram_tensor("bo_c", [128, 32], F32, kind="ExternalInput")
    gam_c = nc.dram_tensor("gam_c", [128, 32], F32, kind="ExternalInput")
    bet_c = nc.dram_tensor("bet_c", [128, 32], F32, kind="ExternalInput")
    ones_bf = nc.dram_tensor("ones_bf", [128, 128], BF16, kind="ExternalInput")
    ident_bf = nc.dram_tensor("ident_bf", [128, 128], BF16, kind="ExternalInput")
    onesrow = nc.dram_tensor("onesrow", [1, 128], F32, kind="ExternalInput")
    eps_in = nc.dram_tensor("eps_in", [128, 1], F32, kind="ExternalInput")

    outT = nc.dram_tensor("outT", [H, TOK], F32, kind="ExternalOutput")
    qkv_scr = nc.dram_tensor("qkv_scr", [24, 128, 32, 128], BF16, kind="Internal")
    wo_scr = nc.dram_tensor("wo_scr", [32, 128, 8, 128], BF16, kind="Internal")
    f1_scr = nc.dram_tensor("f1_scr", [16, 128, 32, 128], BF16, kind="Internal")
    ex_scr = nc.dram_tensor("ex_scr", [16, 128, 16, 128], BF16, kind="Internal")
    up_scr = nc.dram_tensor("up_scr", [64, 128, 32, 128], BF16, kind="Internal")
    f2_scr = nc.dram_tensor("f2_scr", [32, 128, 16, 128], BF16, kind="Internal")
    dn_scr = nc.dram_tensor("dn_scr", [64, 128, 32, 128], BF16, kind="Internal")
    y_scr = nc.dram_tensor("y_scr", [2, NCH, 128, T], F32, kind="Internal")
    r_scr = nc.dram_tensor("r_scr", [T], F32, kind="Internal")
    s_scr = nc.dram_tensor("s_scr", [T], F32, kind="Internal")

    dbg_t = {}
    def _d(name, shape, dt):
        if name in dbg:
            dbg_t[name] = nc.dram_tensor("dbg_" + name, shape, dt,
                                         kind="ExternalOutput")
    _d("K", [8, 128, S], BF16)
    _d("V", [16, 128, D], BF16)
    _d("q", [8, 128, TOK], BF16)
    _d("ao", [2, 8, 128, T], BF16)
    _d("o", [2, NCH, 128, T], BF16)
    _d("y", [2, NCH, 128, T], F32)
    _d("xc3", [NCH, 128, T], BF16)
    _d("srep", [2, 128, T], F32)
    _d("lat", [16, 128, T], BF16)
    _d("moe", [16, 128, T], BF16)
    _d("hh", [16, 128, T], BF16)
    _d("v", [NCH, 128, T], BF16)

    es = contextlib.ExitStack()
    kv_es = contextlib.ExitStack()
    p1_es = contextlib.ExitStack()
    p2_es = contextlib.ExitStack()
    att_es = contextlib.ExitStack()
    ffn_es = contextlib.ExitStack()

    def sbt(stack, name, shape, dt, side="left"):
        return stack.enter_context(nc.sbuf_tensor(name, shape, dt, side=side))

    onb = sbt(es, "onb", [128, 128], BF16)
    idb = sbt(es, "idb", [128, 128], BF16)
    onr = sbt(es, "onr", [1, 128], F32)
    gateb = sbt(es, "gateb", [128, NCH, NE], BF16)
    bqc = sbt(es, "bqc", [128, 8], F32)
    bvc = sbt(es, "bvc", [128, 8], F32)
    boc = sbt(es, "boc", [128, 32], F32)
    gamc = sbt(es, "gamc", [128, 32], F32)
    betc = sbt(es, "betc", [128, 32], F32)
    m_rep = sbt(es, "m_rep", [128, T], F32)
    r_rep = sbt(es, "r_rep", [128, T], F32)
    t_rep = sbt(es, "t_rep", [128, T], F32)
    s_rep = sbt(es, "s_rep", [128, T], F32)
    rcols = sbt(es, "rcols", [128, 4], F32)
    ncol = sbt(es, "ncol", [128, 1], F32)
    rsum = sbt(es, "rsum", [128, 1], F32)
    rrec = sbt(es, "rrec", [128, 1], F32)
    sqr = sbt(es, "sqr", [128, 2, T], BF16)
    EW = 4
    Ltok = sbt(es, "Ltok", [128, 4, NE], F32)
    L2 = sbt(es, "L2", [128, 4, 2 * NE], F32)
    Ltd = sbt(es, "Ltd", [128, 4, NE], F32)
    Rmax = sbt(es, "Rmax", [128, 4, NE - 1], F32)
    s_cols = sbt(es, "s_cols", [128, 4], F32)
    s_row = sbt(es, "s_row", [1, T], F32)
    epsc = sbt(es, "epsc", [128, 1], F32)

    sgx = Sem(es, nc, "sgx")
    sxm = Sem(es, nc, "sxm")
    sxr = Sem(es, nc, "sxr")
    sxl = [Sem(es, nc, "sxl0"), Sem(es, nc, "sxl1")]
    sxs = [Sem(es, nc, "sxs0"), Sem(es, nc, "sxs1")]
    SXW = {}
    for rname in ("rA", "rB", "rF", "rX", "rD"):
        SXW[rname] = [Sem(es, nc, f"sxw_{rname}{i}") for i in range(4)]
    st_ = Sem(es, nc, "st")
    stm = Sem(es, nc, "stm")
    sgc = Sem(es, nc, "sgc")
    sv = Sem(es, nc, "sv")
    ss = Sem(es, nc, "ss")

    ENG = {"v": "vector", "s": "scalar", "g": "gpsimd"}
    SEMS = {"v": sv, "s": ss, "g": sgx}

    psp = es.enter_context(nc.psum_tensor("psp", [128, 8, T], F32))
    ps_a = psp[:, 0:4]
    ps_a_flat = psp[:, 0:4].rearrange("p a b -> p (a b)")
    ps_b = psp[:, 0:2]
    ps_c = psp[:, 6:8]
    ps_pt = [psp[:, 4].bitcast(BF16).rearrange("p (a b) -> p a b", b=128),
             psp[:, 5].bitcast(BF16).rearrange("p (a b) -> p a b", b=128)]
    ps_sm = psp[:, 0, 0:32].rearrange("p (a b) -> p a b", b=NE)
    ps_srep = psp[:, 1]

    xTr = xT[:, :].rearrange("(c p) s -> p c s", p=128)

    def wstream(ring, rname, srcs, lastg_vals, lastg2_vals, scr=None):
        sxw = SXW[rname]
        n = len(srcs)
        nsl = ring.nslots
        i0 = ring.nload
        w0 = [s.n for s in sxw]

        def stores_before(i, slot):
            return sum(1 for j in range(i + 1) if (i0 + j) % nsl == slot)

        def pre(g):
            for j in range(min(nsl, n)):
                sl = (i0 + j) % nsl
                ring.load(g, srcs[j], free=[(sxw[sl], w0[sl])])

        def rest(g):
            for j in range(min(nsl, n), n):
                sl = (i0 + j) % nsl
                fr = [(st_, lastg_vals[j - nsl]),
                      (stm, lastg2_vals[j - nsl])]
                if scr is not None:
                    fr.append((sxw[sl], w0[sl] + 16 * stores_before(j - nsl, sl)))
                else:
                    fr.append((sxw[sl], w0[sl]))
                ring.load(g, srcs[j], free=fr)

        if scr is None:
            return pre, rest, None

        def syncf(s):
            for j in range(n):
                sl = (i0 + j) % nsl
                sem, val = ring.wait(i0 + j)
                s.wait_ge(sem.h, val)
                sxw[sl].inc(s.dma_start(scr[j], ring.buf[:, sl]), 16)
        return pre, rest, syncf

    def split_loads(ring, srcs, free_vals=None):
        npre = ring.nslots

        def pre(g):
            for sap in srcs[:npre]:
                ring.load(g, sap)

        def rest(g):
            for i in range(npre, len(srcs)):
                ring.load(g, srcs[i],
                          free=(free_vals[i] if free_vals else None))
        return pre, rest

    def block(**bodies):
        with nc.Block() as blk:
            for name, fn in bodies.items():
                getattr(blk, name)(fn)

    class _TRec:
        def __init__(self, t):
            self.t = t
            self.last = None

        def _push(self, ins):
            if self.last is not None:
                stm.inc(self.last)
            self.last = ins
            return ins

        def matmul(self, *a, **k):
            return self._push(self.t.matmul(*a, **k))

        def transpose(self, *a, **k):
            return self._push(self.t.transpose(*a, **k))

        def wait_ge(self, *a, **k):
            return self.t.wait_ge(*a, **k)

    def mm_stream(groups, gpsimd_fn=None, sync_fn=None):
        t_base = st_.n
        stm0 = stm.n
        cnt = {k: SEMS[k].n for k in SEMS}
        stage_vals, bank_free, stm_after = [], [], []
        acc = stm0
        for g in groups:
            vals = []
            for (ek, fn, _w) in g["evac"]:
                cnt[ek] += 1
                vals.append((ek, cnt[ek]))
            stage_vals.append(vals)
            bi = g.get("bank_stage")
            if bi is None:
                bi = len(vals) - 1
            bank_free.append(vals[bi] if vals else None)
            acc += max(0, g.get("nmm", 32) - 1)
            stm_after.append(acc)

        def resolve(w):
            out = []
            for item in (w or []):
                if isinstance(item[0], str) and item[0] == "stage":
                    _, gj, si = item
                    ek, val = stage_vals[gj][si]
                    out.append((SEMS[ek], val))
                elif isinstance(item[0], str) and item[0] == "stm_through":
                    out.append((stm, stm_after[item[1]]))
                elif isinstance(item[0], str) and item[0] == "st_done":
                    out.append((st_, t_base + item[1] + 1))
                else:
                    out.append(item)
            return out

        last_by_key = {}
        with nc.Block() as blk:
            @blk.tensor
            def _(t):
                for j, g in enumerate(groups):
                    for (sem, val) in resolve(g.get("waits")):
                        t.wait_ge(sem.h, val)
                    key, nsl = g.get("bank", ("c", 2))
                    hist = last_by_key.setdefault(key, [])
                    if len(hist) >= nsl:
                        jj = hist[-nsl]
                        if bank_free[jj] is not None:
                            ek, val = bank_free[jj]
                            t.wait_ge(SEMS[ek].h, val)
                    hist.append(j)
                    rec = _TRec(t)
                    g["mms"](rec)
                    st_.inc(rec.last)

            for ekey in ("v", "s", "g"):
                stages = [(j, si, fn, w) for j, g in enumerate(groups)
                          for si, (ek, fn, w) in enumerate(g["evac"])
                          if ek == ekey]
                if not stages:
                    continue

                def body(eng, stages=stages, ekey=ekey):
                    for (j, si, fn, w) in stages:
                        if si == 0:
                            eng.wait_ge(st_.h, t_base + j + 1)
                        else:
                            pk, pv = stage_vals[j][si - 1]
                            eng.wait_ge(SEMS[pk].h, pv)
                        for (sem, val) in resolve(w):
                            eng.wait_ge(sem.h, val)
                        SEMS[ekey].inc(fn(eng))
                getattr(blk, ENG[ekey])(body)

            if gpsimd_fn is not None:
                blk.gpsimd(gpsimd_fn)
            if sync_fn is not None:
                blk.sync(sync_fn)

    def stats_block(src_chunks, with_sq=True, xb_wait=None):
        n = len(src_chunks)
        sv0, st0 = sv.n, st_.n

        def tfn(t):
            for (sem, val) in (xb_wait or []):
                t.wait_ge(sem.h, val)
            for c, ap in enumerate(src_chunks):
                t.matmul(ps_b[:, 0], onb[:], ap, start=(c == 0), stop=(c == n - 1))
            if with_sq:
                for c in range(n):
                    t.wait_ge(sv.h, sv0 + c + 1)
                    st_.inc(t.matmul(ps_b[:, 1], onb[:], sqr[:, c % 2],
                                     start=(c == 0), stop=(c == n - 1)))

        def vfn(v):
            for (sem, val) in (xb_wait or []):
                v.wait_ge(sem.h, val)
            for c, ap in enumerate(src_chunks):
                if c >= 2:
                    v.wait_ge(st_.h, st0 + (c - 1))
                sv.inc(v.tensor_mul(sqr[:, c % 2], ap, ap))
        if with_sq:
            block(tensor=tfn, vector=vfn)
        else:
            block(tensor=tfn)

    def ln_finish(with_var=True):
        def vfn(v):
            v.tensor_scalar_mul(m_rep[:], ps_b[:, 0], 1.0 / H)
            if with_var:
                v.drain()
                v.tensor_mul(t_rep[:], m_rep[:], m_rep[:])
                v.drain()
                v.scalar_tensor_tensor(t_rep[:], ps_b[:, 1], 1.0 / H,
                                       t_rep[:], ALU.mult, ALU.subtract)
        block(vector=vfn)
        if with_var:
            block(scalar=lambda s: s.activation(t_rep[:], t_rep[:], AF.Sqrt,
                                                bias=epsc[:]))
            block(vector=lambda v: v.reciprocal(r_rep[:], t_rep[:]))

    def p0_sync(s):
        for dst, src in [(onb, ones_bf), (idb, ident_bf), (onr, onesrow),
                         (bqc, bq_c), (bvc, bv_c), (boc, bo_c),
                         (gamc, gam_c), (betc, bet_c), (epsc, eps_in)]:
            sxm.inc(s.dma_start(dst[:], src[:]), 16)

    def p0_g(g):
        sgx.inc(g.dma_start(gateb[:], Wg[:]), 16)
    block(sync=p0_sync, gpsimd=p0_g)
    c_setup_sx = sxm.n

    Kb = sbt(kv_es, "Kb", [128, 8, S], BF16)
    Vb = sbt(kv_es, "Vb", [128, 16, D], BF16)
    qb = sbt(kv_es, "qb", [128, 8, TOK], BF16)
    C1a = sbt(p1_es, "C1a", [128, NCH, T], BF16, side="right")
    C2a = sbt(p1_es, "C2a", [128, NCH, T], BF16, side="right")
    wr2a = sbt(p1_es, "wr2a", [128, 3, NCH, 128], BF16, side="right")
    rA = Ring(es, nc, "rA", wr2a, 3)

    for stt in range(4):
        t0 = stt * T
        own = stt < 2
        kc0 = stt * 4
        i_w0 = rA.nload

        nq = 8 if own else 0
        if stt == 0:
            srcs = [Wqkv[j] for j in range(24)]
            scr = [qkv_scr[j] for j in range(24)]
        else:
            srcs = ([qkv_scr[dc] for dc in range(8)] if own else []) + \
                   [qkv_scr[8 + dc] for dc in range(8)] + \
                   [qkv_scr[16 + dc] for dc in range(8)]
            scr = None
        lgv, lgv2 = [], []
        pre_ld, rest_ld, sync_wb = wstream(rA, "rA", srcs, lgv, lgv2, scr=scr)

        if stt == 0:
            def ld(g, pre_ld=pre_ld):
                sgx.inc(g.dma_start(C1a[:], xTr[:, :, 0:T]), 16)
                pre_ld(g)
            block(gpsimd=ld)
            c_xb = sgx.n
        else:
            block(gpsimd=pre_ld)
            c_xb = c_xb_next

        stats_block([C1a[:, c] for c in range(NCH)], with_sq=True,
                    xb_wait=[(sgx, c_xb)] + ([(sxm, c_setup_sx)] if stt == 0 else []))
        ln_finish(with_var=True)

        c_rrow = sxr.n + 16
        c_rcols = sxr.n + 32

        c_xcv = sv.n + 1

        def xcv(v):
            sv.inc(v.tensor_sub(C2a[:], C1a[:],
                                m_rep[:].unsqueeze(1).broadcast_to([128, NCH, T])))

        def rsync(s):
            sxr.inc(s.dma_start(r_scr[:], r_rep[0:1, :]), 16)
            s.wait_ge(sxr.h, c_rrow)
            sxr.inc(s.dma_start(rcols[:],
                                r_scr[:].rearrange("(t p) -> p t", p=128),
                                allow_slow_non_contiguous=True), 16)
        block(vector=xcv, sync=rsync)

        groups = []
        jq = 0
        BK = [2, 3, 4, 5, 6, 7]
        if own:
            for dc in range(8):
                j = jq
                bk = BK[j % 6]

                def mk(t, j=j, bk=bk):
                    sl = (i_w0 + j) % 3
                    for hc in range(NCH):
                        t.matmul(psp[:, bk], wr2a[:, sl, hc], C2a[:, hc],
                                 start=(hc == 0), stop=(hc == NCH - 1))

                def ev(v, dc=dc, bk=bk, t0=t0):
                    return v.tensor_mul(qb[:, dc, t0:t0 + T], psp[:, bk],
                                        r_rep[:])
                groups.append({"mms": mk, "evac": [("v", ev, None)],
                               "waits": [rA.wait(i_w0 + j)], "nmm": 32,
                               "bank": ("c", 6)})
                jq += 1
        for dc in range(8):
            j = jq
            bk = BK[j % 6]

            def mk(t, j=j, bk=bk):
                sl = (i_w0 + j) % 3
                for hc in range(NCH):
                    t.matmul(psp[:, bk], wr2a[:, sl, hc], C2a[:, hc],
                             start=(hc == 0), stop=(hc == NCH - 1))

            def ev(v, dc=dc, bk=bk, t0=t0):
                return v.tensor_mul(Kb[:, dc, t0:t0 + T], psp[:, bk],
                                    r_rep[:])
            groups.append({"mms": mk, "evac": [("v", ev, None)],
                           "waits": [rA.wait(i_w0 + j)], "nmm": 32,
                           "bank": ("c", 6)})
            jq += 1
        for dc in range(8):
            vblk = nq + 8 + dc
            for tc in range(4):
                j = jq
                bk = BK[j % 6]

                def mk(t, vblk=vblk, tc=tc, bk=bk):
                    sl = (i_w0 + vblk) % 3
                    for hc in range(NCH):
                        t.matmul(psp[:, bk, 0:128],
                                 C2a[:, hc, tc * 128:(tc + 1) * 128],
                                 wr2a[:, sl, hc],
                                 start=(hc == 0), stop=(hc == NCH - 1))

                def ev(s, dc=dc, tc=tc, bk=bk, kc0=kc0):
                    return s.activation(
                        Vb[:, kc0 + tc, dc * 128:(dc + 1) * 128],
                        psp[:, bk, 0:128], AF.Identity,
                        scale=rcols[:, tc:tc + 1])
                groups.append({"mms": mk, "evac": [("s", ev, [(sxr, c_rcols)])],
                               "waits": [rA.wait(i_w0 + vblk)], "nmm": 32,
                               "bank": ("c", 6)})
                jq += 1

        def lastg(b, nq=nq):
            if b < nq + 8:
                return b
            return nq + 8 + (b - nq - 8) * 4 + 3
        t_base0 = st_.n
        stm0_ = stm.n
        _nmm = [g.get("nmm", 32) for g in groups]
        _cum = [0]
        for v in _nmm:
            _cum.append(_cum[-1] + max(0, v - 1))
        lgv[:] = [t_base0 + lastg(b) + 1 for b in range(len(srcs))]
        lgv2[:] = [stm0_ + _cum[lastg(b) + 1] for b in range(len(srcs))]

        if stt < 3:
            c_xb_next = sgx.n + 16

            def gfn(g, stt=stt, c_xcv=c_xcv, rest_ld=rest_ld):
                g.wait_ge(sv.h, c_xcv)
                sgx.inc(g.dma_start(C1a[:],
                                    xTr[:, :, (stt + 1) * T:(stt + 2) * T]), 16)
                rest_ld(g)
        else:
            def gfn(g, rest_ld=rest_ld):
                rest_ld(g)
        mm_stream(groups, gpsimd_fn=gfn, sync_fn=sync_wb)

    def qbias(s):
        for dc in range(8):
            s.activation(qb[:, dc], qb[:, dc], AF.Identity, bias=bqc[:, dc:dc + 1])
            s.drain()
    block(scalar=qbias)

    def _q1(s):
        for i in range(3):
            s.wait_ge(SXW["rA"][i].h, SXW["rA"][i].n)
    block(sync=_q1)
    p1_es.close()

    if stage == "pass1":
        def taps(s):
            if "K" in dbg_t:
                for dc in range(8):
                    sxm.inc(s.dma_start(dbg_t["K"][dc], Kb[:, dc]), 16)
            if "V" in dbg_t:
                for kc in range(16):
                    sxm.inc(s.dma_start(dbg_t["V"][kc], Vb[:, kc]), 16)
            if "q" in dbg_t:
                for dc in range(8):
                    sxm.inc(s.dma_start(dbg_t["q"][dc], qb[:, dc]), 16)
            s.wait_ge(sxm.h, sxm.n)
        block(sync=taps)
        for stk in (att_es, ffn_es, p2_es, kv_es, es):
            stk.close()
        return nc, dbg_t

    Ab = sbt(p2_es, "Ab", [128, NCH, T], BF16, side="right")
    xf32 = sbt(att_es, "xf32", [128, 2, EW, T], F32, side="right")
    yf32 = sbt(att_es, "yf32", [128, 2, EW, T], F32, side="right")
    wo2 = sbt(att_es, "wo2", [128, 2, 8, 128], BF16, side="right")
    Pb = sbt(att_es, "Pb", [128, 2048], BF16, side="right")
    PTb = sbt(att_es, "PTb", [128, 16, T], BF16, side="right")
    aob = sbt(att_es, "aob", [128, 8, T], BF16, side="right")
    rB = Ring(es, nc, "rB", wo2, 2)

    for ot in range(2):
        tq0 = ot * T
        i_o0 = rB.nload

        if ot == 0:
            osrcs = [Wo[hc] for hc in range(NCH)]
            oscr = [wo_scr[hc] for hc in range(NCH)]
        else:
            osrcs = [wo_scr[hc] for hc in range(NCH)]
            oscr = None
        olgv, olgv2 = [], []
        pre_o, rest_o, sync_owb = wstream(rB, "rB", osrcs, olgv, olgv2, scr=oscr)
        block(gpsimd=pre_o)

        groups = []
        gidx = {}
        s_defs = {}
        t_defs = {}
        for qs in range(4):
            def mk_s(t, qs=qs, tq0=tq0):
                for dc in range(8):
                    for kg in range(4):
                        t.matmul(ps_a[:, kg],
                                 qb[:, dc, tq0 + qs * 128:tq0 + (qs + 1) * 128],
                                 Kb[:, dc, kg * T:(kg + 1) * T],
                                 start=(dc == 0), stop=(dc == 7))
            s_defs[qs] = {
                "mms": mk_s,
                "evac": [
                    ("v", lambda v: v.reduce_max(ncol[:], ps_a_flat,
                                                 axis=AX.X, negate=True), None),
                    ("v", lambda v: v.tensor_scalar_mul(ncol[:], ncol[:],
                                                        1.0 / 32.0), None),
                    ("s", lambda s: s.activation(Pb[:], ps_a_flat, AF.Exp,
                                                 bias=ncol[:], scale=1.0 / 32.0),
                     ([("stm_through", ("T", qs - 1, 1)),
                       ("st_done", ("T", qs - 1, 1))]
                      if qs >= 1 else None)),
                    ("v", lambda v: v.reduce_sum(rsum[:], Pb[:], axis=AX.X),
                     None),
                    ("v", lambda v: v.reciprocal(rrec[:], rsum[:]), None),
                    ("v", lambda v: v.tensor_scalar_mul(Pb[:], Pb[:], rrec[:]),
                     None),
                ],
                "waits": None, "bank": ("a", 1), "bank_stage": 2, "nmm": 32,
            }
            for g2 in range(2):
                def mk_t(t, g2=g2):
                    for jj in range(8):
                        kc = g2 * 8 + jj
                        t.transpose(ps_pt[g2][:, jj],
                                    Pb[:, kc * 128:(kc + 1) * 128], idb[:])

                def ev_pt(v, qs=qs, g2=g2):
                    return v.tensor_copy(
                        PTb[:, g2 * 8:(g2 + 1) * 8, qs * 128:(qs + 1) * 128],
                        ps_pt[g2][:])
                t_defs[(qs, g2)] = {"mms": mk_t, "evac": [("v", ev_pt, None)],
                                    "waits": [("stage", ("S", qs), 5)],
                                    "bank": ("pt", 2), "nmm": 8}
        order = [("S", 0), ("S", 1), ("T", 0, 0), ("T", 0, 1), ("S", 2),
                 ("T", 1, 0), ("T", 1, 1), ("S", 3), ("T", 2, 0), ("T", 2, 1),
                 ("T", 3, 0), ("T", 3, 1)]
        for key in order:
            gidx[key] = len(groups)
            groups.append(s_defs[key[1]] if key[0] == "S" else t_defs[key[1:]])
        for g in groups:
            w = g.get("waits")
            if w:
                g["waits"] = [(("stage", gidx[x[1]], x[2]) if (x[0] == "stage" and isinstance(x[1], tuple)) else x)
                              for x in w]
            ev = []
            for (ek, fn, xw) in g["evac"]:
                if xw:
                    xw = [((y[0], gidx[y[1]]) if (y[0] in ("stm_through", "st_done") and isinstance(y[1], tuple)) else y)
                          for y in xw]
                ev.append((ek, fn, xw))
            g["evac"] = ev

        BKA = [6, 7, 0, 1, 2, 3]
        for dc in range(8):
            bk = BKA[dc % 6]

            def mk_pv(t, dc=dc, bk=bk):
                for kc in range(16):
                    t.matmul(psp[:, bk],
                             Vb[:, kc, dc * 128:(dc + 1) * 128],
                             PTb[:, kc], start=(kc == 0), stop=(kc == 15))

            def ev_pv(s, dc=dc, bk=bk):
                return s.activation(aob[:, dc], psp[:, bk], AF.Identity,
                                    bias=bvc[:, dc:dc + 1])
            gidx[("PV", dc)] = len(groups)
            w = []
            if dc == 0:
                w.append(("stage", gidx[("T", 3, 1)], 0))
            if dc == 2:
                w.append(("stage", gidx[("S", 3)], 2))
            groups.append({
                "mms": mk_pv, "evac": [("s", ev_pv, None)],
                "waits": w or None,
                "bank": ("c", 6), "nmm": 16})
        for hc in range(NCH):
            bk = BKA[(8 + hc) % 6]

            def mk_wo(t, hc=hc, bk=bk):
                sl = (i_o0 + hc) % 2
                for dc in range(8):
                    t.matmul(psp[:, bk], wo2[:, sl, dc], aob[:, dc],
                             start=(dc == 0), stop=(dc == 7))

            def ev_wo(s, hc=hc, bk=bk):
                return s.activation(Ab[:, hc], psp[:, bk], AF.Identity,
                                    bias=boc[:, hc:hc + 1])
            w = [rB.wait(i_o0 + hc)]
            if hc == 0:
                w.append(("stage", gidx[("PV", 7)], 0))
            groups.append({"mms": mk_wo, "evac": [("s", ev_wo, None)],
                           "waits": w, "bank": ("c", 6), "nmm": 8})
        t_base0 = st_.n
        stm0_ = stm.n
        g_wo0 = len(groups) - NCH
        _nmm = [g.get("nmm", 32) for g in groups]
        _cum = [0]
        for v in _nmm:
            _cum.append(_cum[-1] + max(0, v - 1))
        olgv[:] = [t_base0 + g_wo0 + b + 1 for b in range(NCH)]
        olgv2[:] = [stm0_ + _cum[g_wo0 + b + 1] for b in range(NCH)]
        mm_stream(groups, gpsimd_fn=rest_o, sync_fn=sync_owb)

        if "ao" in dbg_t or "o" in dbg_t:
            def tap_ao(s, ot=ot):
                if "ao" in dbg_t:
                    for dc in range(8):
                        sxm.inc(s.dma_start(dbg_t["ao"][ot, dc], aob[:, dc]), 16)
                if "o" in dbg_t:
                    for hc in range(NCH):
                        sxm.inc(s.dma_start(dbg_t["o"][ot, hc], Ab[:, hc]), 16)
                s.wait_ge(sxm.h, sxm.n)
            block(sync=tap_ao)

        stats_block([Ab[:, c] for c in range(NCH)], with_sq=True)
        ln_finish(with_var=True)

        sv0 = sv.n
        cl0 = [sxl[0].n, sxl[1].n]
        cs0 = [sxs[0].n, sxs[1].n]
        m2b = m_rep[:].unsqueeze(1).broadcast_to([128, EW, T])
        r2b = r_rep[:].unsqueeze(1).broadcast_to([128, EW, T])

        def ysync(s, ot=ot, tq0=tq0):
            for k in range(NCH // EW):
                if k >= 2:
                    s.wait_ge(sv.h, sv0 + (k - 1))
                sxl[k % 2].inc(s.dma_start(
                    xf32[:, k % 2],
                    xTr[:, EW * k:EW * (k + 1), tq0:tq0 + T]), 16)
                s.wait_ge(sv.h, sv0 + k + 1)
                sxs[k % 2].inc(s.dma_start(
                    y_scr[ot, EW * k:EW * (k + 1)].transpose([1, 0, 2]),
                    yf32[:, k % 2]), 16)

        def yvec(v):
            for k in range(NCH // EW):
                v.wait_ge(sxs[k % 2].h, cs0[k % 2] + 16 * (k // 2))
                v.tensor_sub(yf32[:, k % 2], Ab[:, EW * k:EW * (k + 1)], m2b)
                v.drain()
                v.tensor_mul(yf32[:, k % 2], yf32[:, k % 2], r2b)
                v.drain()
                v.wait_ge(sxl[k % 2].h, cl0[k % 2] + 16 * (k // 2 + 1))
                sv.inc(v.tensor_add(yf32[:, k % 2], yf32[:, k % 2],
                                    xf32[:, k % 2]))
        block(sync=ysync, vector=yvec)

        if "y" in dbg_t:
            def tap_y(s, ot=ot):
                s.wait_ge(sxs[0].h, sxs[0].n)
                s.wait_ge(sxs[1].h, sxs[1].n)
                for hc in range(NCH):
                    sxm.inc(s.dma_start(dbg_t["y"][ot, hc], y_scr[ot, hc]), 16)
                s.wait_ge(sxm.h, sxm.n)
            block(sync=tap_y)

    def _q2(s):
        for i in range(2):
            s.wait_ge(SXW["rB"][i].h, SXW["rB"][i].n)
    block(sync=_q2)
    att_es.close()
    kv_es.close()

    if stage == "attn":
        def _fw(s):
            s.wait_ge(sxs[0].h, sxs[0].n)
            s.wait_ge(sxs[1].h, sxs[1].n)
        block(sync=_fw)
        for stk in (ffn_es, p2_es, es):
            stk.close()
        return nc, dbg_t

    C1f = sbt(ffn_es, "C1f", [128, NCH, T], BF16, side="right")
    C2f = sbt(ffn_es, "C2f", [128, NCH, T], BF16, side="right")
    wrf = sbt(ffn_es, "wrf", [128, 3, NCH, 128], BF16, side="right")
    wxf = sbt(ffn_es, "wxf", [128, 2, 2, 16, 128], BF16, side="right")
    rF = Ring(es, nc, "rF", wrf, 3)
    rX = Ring(es, nc, "rX", wxf, 2)

    def hchunk(ic):
        return C1f[:, ic] if ic < NCH else Ab[:, ic - NCH]

    for ot in range(2):
        tq0 = ot * T
        lm_es = contextlib.ExitStack()
        latb = sbt(lm_es, f"latb{ot}", [128, 16, T], BF16, side="right")
        moeb = sbt(lm_es, f"moeb{ot}", [128, 16, T], BF16, side="right")

        c_ybf = sgx.n + 16

        def ldy(g, ot=ot):
            g.wait_ge(sxs[0].h, sxs[0].n)
            g.wait_ge(sxs[1].h, sxs[1].n)
            sgx.inc(g.dma_start(C1f[:], y_scr[ot].rearrange("c p t -> p c t")), 16)
        block(gpsimd=ldy)

        stats_block([C1f[:, c] for c in range(NCH)], with_sq=False,
                    xb_wait=[(sgx, c_ybf)])
        ln_finish(with_var=False)

        def xc3v(v):
            v.tensor_sub(C2f[:], C1f[:],
                         m_rep[:].unsqueeze(1).broadcast_to([128, NCH, T]))
        block(vector=xc3v)

        if "xc3" in dbg_t and ot == 0:
            def tapx(s):
                for hc in range(NCH):
                    sxm.inc(s.dma_start(dbg_t["xc3"][hc], C2f[:, hc]), 16)
                s.wait_ge(sxm.h, sxm.n)
            block(sync=tapx)

        def mk_gate(t):
            for tc in range(4):
                for hc in range(NCH):
                    t.matmul(ps_sm[:, tc], C2f[:, hc, tc * 128:(tc + 1) * 128],
                             gateb[:, hc], start=(hc == 0), stop=(hc == NCH - 1))
        mm_stream([{"mms": mk_gate,
                    "evac": [("v", lambda v: v.tensor_copy(Ltok[:], ps_sm[:]),
                              None)],
                    "waits": None, "bank": ("a", 1), "nmm": 128}])

        def top2(v):
            v.tensor_copy(L2[:, :, 0:NE], Ltok[:])
            v.tensor_copy(L2[:, :, NE:2 * NE], Ltok[:])
            v.drain()
            for d in range(1, NE):
                v.tensor_add(Ltd[:], L2[:, :, 0:NE], L2[:, :, d:d + NE])
                v.drain()
                v.reduce_max(Rmax[:, :, d - 1:d], Ltd[:], axis=AX.X)
                v.drain()
            v.reduce_max(s_cols[:].unsqueeze(2), Rmax[:], axis=AX.X)
        block(vector=top2)

        c_scols = sxr.n + 16
        c_srow = sxr.n + 32

        def s_sync(s):
            sxr.inc(s.dma_start(s_scr[:].rearrange("(t p) -> p t", p=128),
                                s_cols[:], allow_slow_non_contiguous=True), 16)
            s.wait_ge(sxr.h, c_scols)
            sxr.inc(s.dma_start(s_row[:], s_scr[:].unsqueeze(0)), 16)
        block(sync=s_sync)

        def mk_srep(t):
            t.wait_ge(sxr.h, c_srow)
            t.matmul(ps_srep[:], onr[:], s_row[:], start=True, stop=True)
        mm_stream([{"mms": mk_srep,
                    "evac": [("v", lambda v: v.tensor_copy(s_rep[:], ps_srep[:]),
                              None)],
                    "waits": None, "bank": ("b", 1), "nmm": 1}])

        if "srep" in dbg_t:
            def taps2(s, ot=ot):
                sxm.inc(s.dma_start(dbg_t["srep"][ot], s_rep[:]), 16)
                s.wait_ge(sxm.h, sxm.n)
            block(sync=taps2)

        i_f0 = rF.nload

        f1srcs = [Wf1[lc] for lc in range(16)] if ot == 0 else \
                 [f1_scr[lc] for lc in range(16)]
        f1lgv, f1lgv2 = [], []
        pre_f1, rest_f1, sync_f1wb = wstream(
            rF, "rF", f1srcs, f1lgv, f1lgv2,
            scr=([f1_scr[lc] for lc in range(16)] if ot == 0 else None))
        block(gpsimd=pre_f1)

        groups = []
        BK = [2, 3, 4, 5, 6, 7]
        for lc in range(16):
            bk = BK[lc % 6]

            def mk(t, lc=lc, bk=bk):
                sl = (i_f0 + lc) % 3
                for hc in range(NCH):
                    t.matmul(psp[:, bk], wrf[:, sl, hc], C2f[:, hc],
                             start=(hc == 0), stop=(hc == NCH - 1))

            def ev(v, lc=lc, bk=bk):
                return v.tensor_copy(latb[:, lc], psp[:, bk])
            groups.append({"mms": mk, "evac": [("v", ev, None)],
                           "waits": [rF.wait(i_f0 + lc)], "nmm": 32,
                           "bank": ("c", 6)})
        t_base0 = st_.n
        f1lgv[:] = [t_base0 + b + 1 for b in range(16)]
        f1lgv2[:] = [stm.n + 31 * (b + 1) for b in range(16)]
        mm_stream(groups, gpsimd_fn=rest_f1, sync_fn=sync_f1wb)

        i_x0 = rX.nload

        def _pair(dram, p):
            return dram[2 * p:2 * p + 2].transpose([1, 0, 2, 3])
        exsrcs = [_pair(Wex, p) for p in range(8)] if ot == 0 else \
                 [_pair(ex_scr, p) for p in range(8)]
        exlgv, exlgv2 = [], []
        pre_ex, rest_ex, sync_exwb = wstream(
            rX, "rX", exsrcs, exlgv, exlgv2,
            scr=([_pair(ex_scr, p) for p in range(8)] if ot == 0 else None))
        block(gpsimd=pre_ex)

        groups = []
        for mc in range(16):
            bk = BK[mc % 6]

            def mk(t, mc=mc, bk=bk):
                sl = (i_x0 + mc // 2) % 2
                for lc in range(16):
                    t.matmul(psp[:, bk], wxf[:, sl, mc % 2, lc], latb[:, lc],
                             start=(lc == 0), stop=(lc == 15))

            def ev(v, mc=mc, bk=bk):
                return v.tensor_mul(moeb[:, mc], psp[:, bk], s_rep[:])
            groups.append({"mms": mk, "evac": [("v", ev, None)],
                           "waits": [rX.wait(i_x0 + mc // 2)], "nmm": 16,
                           "bank": ("c", 6)})
        t_base0 = st_.n
        exlgv[:] = [t_base0 + 2 * p + 2 for p in range(8)]
        exlgv2[:] = [stm.n + 15 * (2 * p + 2) for p in range(8)]
        mm_stream(groups, gpsimd_fn=rest_ex, sync_fn=sync_exwb)

        if "lat" in dbg_t and ot == 0:
            def tapl(s):
                for lc in range(16):
                    sxm.inc(s.dma_start(dbg_t["lat"][lc], latb[:, lc]), 16)
                    sxm.inc(s.dma_start(dbg_t["moe"][lc], moeb[:, lc]), 16)
                s.wait_ge(sxm.h, sxm.n)
            block(sync=tapl)

        i_u0 = rF.nload

        upsrcs = [Wup[ic] for ic in range(64)] if ot == 0 else \
                 [up_scr[ic] for ic in range(64)]
        uplgv, uplgv2 = [], []
        pre_up, rest_up, sync_upwb = wstream(
            rF, "rF", upsrcs, uplgv, uplgv2,
            scr=([up_scr[ic] for ic in range(64)] if ot == 0 else None))
        block(gpsimd=pre_up)

        groups = []
        for ic in range(64):
            bk = BK[ic % 6]

            def mk(t, ic=ic, bk=bk):
                sl = (i_u0 + ic) % 3
                for hc in range(NCH):
                    t.matmul(psp[:, bk], wrf[:, sl, hc], C2f[:, hc],
                             start=(hc == 0), stop=(hc == NCH - 1))

            def ev1(s, ic=ic, bk=bk):
                return s.activation(sqr[:, ic % 2], psp[:, bk], AF.Relu)

            def ev2(v, ic=ic):
                return v.tensor_mul(hchunk(ic), sqr[:, ic % 2], sqr[:, ic % 2])
            groups.append({"mms": mk,
                           "evac": [("s", ev1,
                                     ([("stage", ic - 2, 1)] if ic >= 2 else None)),
                                    ("v", ev2, None)],
                           "waits": [rF.wait(i_u0 + ic)], "nmm": 32,
                           "bank": ("c", 6), "bank_stage": 0})
        t_base0 = st_.n
        uplgv[:] = [t_base0 + b + 1 for b in range(64)]
        uplgv2[:] = [stm.n + 31 * (b + 1) for b in range(64)]
        mm_stream(groups, gpsimd_fn=rest_up, sync_fn=sync_upwb)

        if "hh" in dbg_t and ot == 0:
            def taph(s):
                for ic in range(16):
                    sxm.inc(s.dma_start(dbg_t["hh"][ic], hchunk(ic * 4)), 16)
                s.wait_ge(sxm.h, sxm.n)
            block(sync=taph)

        i_x1 = rX.nload

        f2srcs = [_pair(Wf2, p) for p in range(16)] if ot == 0 else \
                 [_pair(f2_scr, p) for p in range(16)]
        f2lgv, f2lgv2 = [], []
        pre_f2, rest_f2, sync_f2wb = wstream(
            rX, "rX", f2srcs, f2lgv, f2lgv2,
            scr=([_pair(f2_scr, p) for p in range(16)] if ot == 0 else None))
        block(gpsimd=pre_f2)

        groups = []
        for oc in range(NCH):
            bk = BK[oc % 6]

            def mk(t, oc=oc, bk=bk):
                sl = (i_x1 + oc // 2) % 2
                for lc in range(16):
                    t.matmul(psp[:, bk], wxf[:, sl, oc % 2, lc], moeb[:, lc],
                             start=(lc == 0), stop=(lc == 15))

            def ev(v, oc=oc, bk=bk):
                return v.tensor_copy(C2f[:, oc], psp[:, bk])
            groups.append({"mms": mk, "evac": [("v", ev, None)],
                           "waits": [rX.wait(i_x1 + oc // 2)], "nmm": 16,
                           "bank": ("c", 6)})
        t_base0 = st_.n
        f2lgv[:] = [t_base0 + 2 * p + 2 for p in range(16)]
        f2lgv2[:] = [stm.n + 15 * (2 * p + 2) for p in range(16)]
        mm_stream(groups, gpsimd_fn=rest_f2, sync_fn=sync_f2wb)

        lm_es.close()
        dn_es = contextlib.ExitStack()
        wdf = sbt(dn_es, f"wdf{ot}", [128, 4, NCH, 128], BF16, side="right")
        rD = Ring(es, nc, f"rD{ot}", wdf, 4)
        SXW[f"rD{ot}"] = SXW["rD"]

        i_d0 = rD.nload

        dnsrcs = [(Wdn if ot == 0 else dn_scr)[oc * 2 + half]
                  for oc in range(NCH) for half in range(2)]
        dnlgv, dnlgv2 = [], []
        pre_dn, rest_dn, sync_dnwb = wstream(
            rD, f"rD{ot}", dnsrcs, dnlgv, dnlgv2,
            scr=([dn_scr[j] for j in range(64)] if ot == 0 else None))
        block(gpsimd=pre_dn)

        groups = []
        for oc in range(NCH):
            bk = BK[oc % 6]

            def mk(t, oc=oc, bk=bk):
                for half in range(2):
                    sl = (i_d0 + 2 * oc + half) % 4
                    for j in range(NCH):
                        t.matmul(psp[:, bk], wdf[:, sl, j],
                                 hchunk(half * NCH + j),
                                 start=(half == 0 and j == 0),
                                 stop=(half == 1 and j == NCH - 1))

            def ev(v, oc=oc, bk=bk):
                return v.tensor_add(C2f[:, oc], psp[:, bk], C2f[:, oc])
            groups.append({"mms": mk, "evac": [("v", ev, None)],
                           "waits": [rD.wait(i_d0 + 2 * oc), rD.wait(i_d0 + 2 * oc + 1)],
                           "bank": ("c", 6), "nmm": 64})
        t_base0 = st_.n
        dnlgv[:] = [t_base0 + b // 2 + 1 for b in range(64)]
        dnlgv2[:] = [stm.n + 63 * (b // 2 + 1) for b in range(64)]
        mm_stream(groups, gpsimd_fn=rest_dn, sync_fn=sync_dnwb)

        def _q3(s):
            for i in range(4):
                s.wait_ge(SXW["rD"][i].h, SXW["rD"][i].n)
        block(sync=_q3)
        dn_es.close()
        ep_es = contextlib.ExitStack()
        xf32 = sbt(ep_es, f"xf32f{ot}", [128, 2, EW, T], F32, side="right")
        yf32 = sbt(ep_es, f"yf32f{ot}", [128, 2, EW, T], F32, side="right")

        if "v" in dbg_t and ot == 0:
            def tapv(s):
                for oc in range(NCH):
                    sxm.inc(s.dma_start(dbg_t["v"][oc], C2f[:, oc]), 16)
                s.wait_ge(sxm.h, sxm.n)
            block(sync=tapv)

        stats_block([C2f[:, c] for c in range(NCH)], with_sq=True)
        ln_finish(with_var=True)

        sv0, ss0 = sv.n, ss.n
        m4b = m_rep[:].unsqueeze(1).broadcast_to([128, EW, T])
        r4b = r_rep[:].unsqueeze(1).broadcast_to([128, EW, T])

        def t4v(v):
            for k in range(NCH // EW):
                if k >= 2:
                    v.wait_ge(ss.h, ss0 + EW * (k - 1))
                v.tensor_sub(yf32[:, k % 2], C2f[:, EW * k:EW * (k + 1)], m4b)
                v.drain()
                sv.inc(v.tensor_mul(yf32[:, k % 2], yf32[:, k % 2], r4b))

        def t4s(s):
            for k in range(NCH // EW):
                s.wait_ge(sv.h, sv0 + k + 1)
                for j in range(EW):
                    oc = EW * k + j
                    ss.inc(s.activation(C1f[:, oc], yf32[:, k % 2, j],
                                        AF.Identity,
                                        scale=gamc[:, oc:oc + 1],
                                        bias=betc[:, oc:oc + 1]))
        block(vector=t4v, scalar=t4s)

        stats_block([C1f[:, c] for c in range(NCH)], with_sq=True)
        ln_finish(with_var=True)

        sv0 = sv.n
        cl0 = [sxl[0].n, sxl[1].n]
        cs0 = [sxs[0].n, sxs[1].n]
        m5b = m_rep[:].unsqueeze(1).broadcast_to([128, EW, T])
        r5b = r_rep[:].unsqueeze(1).broadcast_to([128, EW, T])

        def fsync(s, ot=ot, tq0=tq0):
            for k in range(NCH // EW):
                if k >= 2:
                    s.wait_ge(sv.h, sv0 + (k - 1))
                sxl[k % 2].inc(s.dma_start(
                    xf32[:, k % 2],
                    y_scr[ot, EW * k:EW * (k + 1)].transpose([1, 0, 2])), 16)
                s.wait_ge(sv.h, sv0 + k + 1)
                sxs[k % 2].inc(s.dma_start(
                    outT[EW * k * 128:EW * (k + 1) * 128, tq0:tq0 + T]
                    .rearrange("(c p) t -> p c t", p=128),
                    yf32[:, k % 2]), 16)

        def fvec(v):
            for k in range(NCH // EW):
                v.wait_ge(sxs[k % 2].h, cs0[k % 2] + 16 * (k // 2))
                v.tensor_sub(yf32[:, k % 2], C1f[:, EW * k:EW * (k + 1)], m5b)
                v.drain()
                v.tensor_mul(yf32[:, k % 2], yf32[:, k % 2], r5b)
                v.drain()
                v.wait_ge(sxl[k % 2].h, cl0[k % 2] + 16 * (k // 2 + 1))
                sv.inc(v.tensor_add(yf32[:, k % 2], yf32[:, k % 2],
                                    xf32[:, k % 2]))
        block(sync=fsync, vector=fvec)
        block(sync=lambda s: (s.wait_ge(sxs[0].h, sxs[0].n),
                              s.wait_ge(sxs[1].h, sxs[1].n),
                              s.wait_ge(sxl[0].h, sxl[0].n),
                              s.wait_ge(sxl[1].h, sxl[1].n)))
        ep_es.close()

    def _final_wait(s):
        s.wait_ge(sxs[0].h, sxs[0].n)
        s.wait_ge(sxs[1].h, sxs[1].n)
    block(sync=_final_wait)

    for stk in (ffn_es, p2_es, es):
        stk.close()
    return nc, dbg_t


_TW_CACHE = {}


def _tile4(W):
    W = np.asarray(W)
    K, N = W.shape
    return np.ascontiguousarray(
        W.reshape(K // 128, 128, N // 128, 128).transpose(2, 1, 0, 3))


def _tiled_weights(inp):
    key = id(inp.get("L0_attn_Wqkv"))
    if key in _TW_CACHE:
        return _TW_CACHE[key]
    dn = _tile4(inp["L0_ffn_down_w"])
    dn = np.ascontiguousarray(
        dn.reshape(32, 128, 2, 32, 128).transpose(0, 2, 1, 3, 4)
    ).reshape(64, 128, 32, 128)
    g = np.asarray(inp["L0_ffn_gate_w"])
    gt = np.ascontiguousarray(g.reshape(32, 128, 8).transpose(1, 0, 2))
    tw = {
        "Wqkv": _tile4(inp["L0_attn_Wqkv"]),
        "Wo": _tile4(inp["L0_attn_Wo"]),
        "Wup": _tile4(inp["L0_ffn_up_w"]),
        "Wdn": dn,
        "Wf1": _tile4(inp["L0_ffn_fc1_w"]),
        "Wf2": _tile4(inp["L0_ffn_fc2_w"]),
        "Wex": _tile4(inp["L0_ffn_expert_w"]),
        "Wg": gt,
    }
    _TW_CACHE.clear()
    _TW_CACHE[key] = tw
    return tw


def make_inputs(inp, core):
    b, half = core // 2, core % 2
    x = np.asarray(inp["L0_attn_x"])
    xb_ = x[b]
    rot = np.concatenate([xb_[half * TOK:(half + 1) * TOK],
                          xb_[(1 - half) * TOK:(2 - half) * TOK]], axis=0)
    bqkv = np.asarray(inp["L0_attn_bqkv"])
    tw = _tiled_weights(inp)
    return {
        "xT": np.ascontiguousarray(rot.T),
        **tw,
        "bq_c": np.ascontiguousarray(bqkv[0:D].reshape(8, 128).T),
        "bv_c": np.ascontiguousarray(bqkv[2 * D:3 * D].reshape(8, 128).T),
        "bo_c": np.ascontiguousarray(
            np.asarray(inp["L0_attn_bo"]).reshape(32, 128).T),
        "gam_c": np.ascontiguousarray(
            np.asarray(inp["L0_ffn_ln_gamma"]).reshape(32, 128).T),
        "bet_c": np.ascontiguousarray(
            np.asarray(inp["L0_ffn_ln_beta"]).reshape(32, 128).T),
        "ones_bf": np.ones((128, 128), bf16),
        "ident_bf": np.eye(128, dtype=bf16),
        "onesrow": np.ones((1, 128), np.float32),
        "eps_in": np.full((128, 1), EPS, np.float32),
    }


from concourse.bass_utils import run_bass_kernel_spmd

_PROG_CACHE = {}


def kernel(L0_attn_x, L0_attn_Wqkv, L0_attn_bqkv, L0_attn_Wo, L0_attn_bo,
           L0_ffn_gate_w, L0_ffn_up_w, L0_ffn_down_w, L0_ffn_fc1_w,
           L0_ffn_fc2_w, L0_ffn_expert_w, L0_ffn_ln_gamma, L0_ffn_ln_beta):
    inp = {
        "L0_attn_x": L0_attn_x, "L0_attn_Wqkv": L0_attn_Wqkv,
        "L0_attn_bqkv": L0_attn_bqkv, "L0_attn_Wo": L0_attn_Wo,
        "L0_attn_bo": L0_attn_bo, "L0_ffn_gate_w": L0_ffn_gate_w,
        "L0_ffn_up_w": L0_ffn_up_w, "L0_ffn_down_w": L0_ffn_down_w,
        "L0_ffn_fc1_w": L0_ffn_fc1_w, "L0_ffn_fc2_w": L0_ffn_fc2_w,
        "L0_ffn_expert_w": L0_ffn_expert_w,
        "L0_ffn_ln_gamma": L0_ffn_ln_gamma, "L0_ffn_ln_beta": L0_ffn_ln_beta,
    }
    if "nc" not in _PROG_CACHE:
        _PROG_CACHE["nc"] = build_program(stage="full")[0]
    nc = _PROG_CACHE["nc"]
    in_maps = [make_inputs(inp, c) for c in range(8)]
    res = run_bass_kernel_spmd(nc, in_maps, core_ids=list(range(8)))
    out = np.empty((B, S, H), np.float32)
    for c in range(8):
        b, half = c // 2, c % 2
        out[b, half * TOK:(half + 1) * TOK, :] = res.results[c]["outT"].T
    return out
